# version 35
# baseline (speedup 1.0000x reference)
import contextlib
import numpy as np
import ml_dtypes

import concourse.bass as bass
import concourse.mybir as mybir
from concourse.bass_utils import run_bass_kernel_spmd

F32 = mybir.dt.float32
BF16 = mybir.dt.bfloat16
AF = mybir.ActivationFunctionType
ALU = mybir.AluOpType
AX = mybir.AxisListType

MAGIC = float(3 * 2**22)
LN_EPS = 1e-5
Z_EPS = 1e-6
D = 1024
H = 16
FF = 4096
I_FF = 2816
NCORES = 8

ENGINES = ("sync", "tensor", "vector", "scalar", "gpsimd")


class Em:

    def __init__(self, nc):
        self.nc = nc
        self.steps = {e: [] for e in ENGINES}
        self.semctx = contextlib.ExitStack()
        self.vals = {}
        self.waited = {e: {} for e in ENGINES}

    def sem(self, name):
        h = self.semctx.enter_context(self.nc.semaphore(name))
        self.vals[name] = 0
        return h

    def _wait(self, eng, ev):
        semh, val = ev
        if self.waited[eng].get(semh.name, 0) >= val:
            return
        self.waited[eng][semh.name] = val
        self.steps[eng].append(lambda e, s=semh, v=val: e.wait_ge(s, v))

    def op(self, eng, fn, after=(), inc=None):
        for ev in after:
            if ev is not None:
                self._wait(eng, ev)
        ev_out = None
        dr = eng in ("scalar", "gpsimd")
        if inc is not None:
            self.vals[inc.name] += 1
            ev_out = (inc, self.vals[inc.name])

            def step(e, fn=fn, inc=inc, dr=dr):
                if dr:
                    e.drain()
                last = fn(e)
                assert last is not None, "op with inc= must return the last instruction"
                last.then_inc(inc, 1)
            self.steps[eng].append(step)
        else:
            def step0(e, fn=fn, dr=dr):
                if dr:
                    e.drain()
                fn(e)
            self.steps[eng].append(step0)
        return ev_out

    def dma(self, eng, out, in_, stream, after=(), transpose=False, serial=False):
        if serial:
            self._wait(eng, (stream, self.vals[stream.name]))
        for ev in after:
            if ev is not None:
                self._wait(eng, ev)
        self.vals[stream.name] += 16
        val = self.vals[stream.name]

        def step(e, out=out, in_=in_, stream=stream, transpose=transpose):
            if transpose:
                e.dma_start_transpose(out=out, in_=in_).then_inc(stream, 16)
            else:
                e.dma_start(out=out, in_=in_).then_inc(stream, 16)
        self.steps[eng].append(step)
        return (stream, val)

    def drain(self, eng):
        self.steps[eng].append(lambda e: e.drain())

    def stream_ev(self, stream):
        return (stream, self.vals[stream.name])

    def barrier(self, gate, per_engine_events):
        tgt = self.vals[gate.name] + 5
        for eng in ENGINES:
            for ev in per_engine_events.get(eng, ()):
                if ev is not None:
                    self._wait(eng, ev)
            self.vals[gate.name] += 1
            self.steps[eng].append(lambda e, g=gate: e.sem_inc(g, 1))
            self._wait(eng, (gate, tgt))

    def replay(self, blk):
        for eng in ENGINES:
            body = getattr(blk, eng)
            steps = self.steps[eng]

            def run(e, steps=steps):
                for s in steps:
                    s(e)
            body(run)


def build(S=4096, n_cores=NCORES, dbg=False):
    TS = 512
    NT = S // TS
    CPT = TS // 128
    NCH = S // 128
    DC = D // 128
    FFC = FF // 128
    ICC = I_FF // 128
    TS4 = 256
    NT4 = S // TS4
    CPT4 = TS4 // 128

    nc = bass.Bass("TRN2", target_bir_lowering=False)

    dx = nc.dram_tensor("x", [S, D], F32, kind="ExternalInput").ap()
    dwq = nc.dram_tensor("wq", [D, D], F32, kind="ExternalInput").ap()
    dwk = nc.dram_tensor("wk", [D, D], F32, kind="ExternalInput").ap()
    dwv = nc.dram_tensor("wv", [D, D], F32, kind="ExternalInput").ap()
    dwo = nc.dram_tensor("wo", [D, D], F32, kind="ExternalInput").ap()
    dbq = nc.dram_tensor("bq", [D], F32, kind="ExternalInput").ap()
    dbk = nc.dram_tensor("bk", [D], F32, kind="ExternalInput").ap()
    dbv = nc.dram_tensor("bv", [D], F32, kind="ExternalInput").ap()
    dbo = nc.dram_tensor("bo", [D], F32, kind="ExternalInput").ap()
    dw1 = nc.dram_tensor("w1", [D, FF], F32, kind="ExternalInput").ap()
    db1 = nc.dram_tensor("b1", [FF], F32, kind="ExternalInput").ap()
    dw2 = nc.dram_tensor("w2", [FF, D], F32, kind="ExternalInput").ap()
    db2 = nc.dram_tensor("b2", [D], F32, kind="ExternalInput").ap()
    dg1 = nc.dram_tensor("g1", [D], F32, kind="ExternalInput").ap()
    dbe1 = nc.dram_tensor("be1", [D], F32, kind="ExternalInput").ap()
    dg2 = nc.dram_tensor("g2", [D], F32, kind="ExternalInput").ap()
    dbe2 = nc.dram_tensor("be2", [D], F32, kind="ExternalInput").ap()
    dclip = nc.dram_tensor("clip2", [1, 2], F32, kind="ExternalInput").ap()
    dwg = nc.dram_tensor("wg", [I_FF, D], F32, kind="ExternalInput").ap()
    dwu = nc.dram_tensor("wu", [I_FF, D], F32, kind="ExternalInput").ap()
    dwd = nc.dram_tensor("wd", [D, I_FF], F32, kind="ExternalInput").ap()
    dout = nc.dram_tensor("out", [S, D], F32, kind="ExternalOutput").ap()

    identf = nc.inline_tensor(np.eye(128, dtype=np.float32), "identf").ap()
    bc2_np = np.zeros((2, 128), dtype=ml_dtypes.bfloat16)
    bc2_np[0, 0:64] = 1.0
    bc2_np[1, 64:128] = 1.0
    dbc2 = nc.inline_tensor(bc2_np, "bc2c").ap()
    DCl = D // 128
    sel_np = np.zeros((16, DCl * 128), dtype=ml_dtypes.bfloat16)
    for occ in range(DCl):
        for pp in range(128):
            sel_np[2 * occ + pp // 64, occ * 128 + pp] = 1.0
    dsel = nc.inline_tensor(sel_np, "zselc").ap()

    ko = dict(kind="ExternalOutput") if dbg else {}
    hxnf = nc.dram_tensor("hxnf", [S, D], F32, **ko).ap()
    hxnb = nc.dram_tensor("hxnb", [D, S], BF16, **ko).ap()
    hlnc1 = nc.dram_tensor("hlnc1", [S, D], BF16, **ko).ap()
    hp = nc.dram_tensor("hp", [I_FF, S], F32, **ko).ap()
    hs = nc.dram_tensor("hs", [S, D], F32, **ko).ap()
    hwgq = nc.dram_tensor("hwgq", [I_FF, D], BF16, **ko).ap()
    hwuq = nc.dram_tensor("hwuq", [I_FF, D], BF16, **ko).ap()
    hwdq = nc.dram_tensor("hwdq", [D, I_FF], BF16, **ko).ap()
    hb1p = nc.dram_tensor("hb1p", [FF], BF16).ap()
    cc_space = "Shared" if n_cores > 4 else "Local"
    ccx_in = nc.dram_tensor("ccx_in", [1, 16], F32).ap()
    ccx_out = nc.dram_tensor("ccx_out", [1, 16], F32, addr_space=cc_space).ap()
    ccd_in = nc.dram_tensor("ccd_in", [1, 16], F32).ap()
    ccd_out = nc.dram_tensor("ccd_out", [1, 16], F32, addr_space=cc_space).ap()
    if dbg:
        hsc = nc.dram_tensor("hsc", [1, 16], F32, kind="ExternalOutput").ap()
        hrfm = nc.dram_tensor("hrfm", [128, 8 * 512], mybir.dt.bfloat16, kind="ExternalOutput").ap()

    ctx = contextlib.ExitStack()
    E = Em(nc)

    def sb(name, shape, dtype, stack=None):
        return (stack or ctx).enter_context(nc.sbuf_tensor(name, list(shape), dtype))

    def psum(name, shape, dtype=F32, stack=None):
        return (stack or ctx).enter_context(nc.psum_tensor(name, list(shape), dtype))

    def tm_view(hbm, t0, n):
        return hbm[t0:t0 + n, :].rearrange("(c p) d -> p c d", p=128)

    with ctx:
        c_id = sb("c_id", [128, 128], F32)
        c_id_bf = sb("c_id_bf", [128, 128], BF16)
        c_ones_bf = sb("c_ones_bf", [1, 512], BF16)
        c_ones_f = sb("c_ones_f", [1, 512], F32)
        c_bc2 = sb("c_bc2", [2, 128], BF16)
        r_bk = sb("r_bk", [1, D], BF16)
        r_bv = sb("r_bv", [1, D], BF16)
        r_bo = sb("r_bo", [1, D], BF16)
        r_b2pp = sb("r_b2pp", [1, D], BF16)
        bq_cols = sb("bq_cols", [128, DC], F32)
        g1cols = sb("g1cols", [128, DC], F32)
        be1cols = sb("be1cols", [128, DC], F32)
        g1b = sb("g1b", [128, D], F32)
        g2b = sb("g2b", [128, D], F32)
        be2b = sb("be2b", [128, D], F32)
        sc_clip = sb("sc_clip", [1, 2], F32)
        sc_sx = sb("sc_sx", [128, 1], F32)
        sc_rsx = sb("sc_rsx", [128, 1], F32)
        sc_sd = sb("sc_sd", [128, 1], F32)
        sc_rsd = sb("sc_rsd", [128, 1], F32)
        amax_sl = sb("amax_sl", [128, NCH], F32)
        amax_red = sb("amax_red", [128, 1], F32)
        pmax_sl = sb("pmax_sl", [128, NT * ICC], F32)
        pmax_red = sb("pmax_red", [128, 1], F32)
        sc_tmp = sb("sc_tmp", [1, 16], F32)
        c_magic = sb("c_magic", [128, 1], F32)
        c_nmagic = sb("c_nmagic", [128, 1], F32)
        s_qkvo = contextlib.ExitStack()
        c_sel = sb("c_sel", [16, DC * 128], BF16, stack=s_qkvo)
        kvbd = sb("kvbd", [128, DC * 128], BF16, stack=s_qkvo)
        zlhs = sb("zlhs", [128, DC * 16], BF16, stack=s_qkvo)
        wq_bf = sb("wq_bf", [128, DC * D], BF16, stack=s_qkvo)
        wk_bf = sb("wk_bf", [128, DC * D], BF16, stack=s_qkvo)
        wv_bf = sb("wv_bf", [128, DC * D], BF16, stack=s_qkvo)
        wo_bf = sb("wo_bf", [128, DC * D], BF16, stack=s_qkvo)

        GATE = E.sem("gate")
        CT = E.sem("ct")
        CV = E.sem("cv")
        CS = E.sem("cs")
        CC = E.sem("cc")
        CG = E.sem("cg")
        sL = [E.sem(f"sl{i}") for i in range(2)]
        sLb = [E.sem(f"slb{i}") for i in range(2)]
        sLc = [E.sem(f"slc{i}") for i in range(2)]
        sLb2 = [E.sem(f"slb2{i}") for i in range(2)]
        sSp = [E.sem(f"ssp{i}") for i in range(3)]
        sSp2 = [E.sem(f"ssp2{i}") for i in range(3)]
        sW = [E.sem(f"sw{i}") for i in range(3)]
        sWs = [E.sem(f"sws{i}") for i in range(3)]
        sWt = E.sem("swt")
        sWt2 = E.sem("swt2")
        sMisc = E.sem("smisc")
        sCc = E.sem("scc")
        sOut = [E.sem(f"souts{i}") for i in range(2)]
        sPf = [E.sem(f"spf{i}") for i in range(6)]
        sW4 = [E.sem(f"sw4{i}") for i in range(3)]

        blk = ctx.enter_context(nc.Block())

        p0 = contextlib.ExitStack()
        g1row = sb("g1row", [1, D], F32, stack=p0)
        g2row = sb("g2row", [1, D], F32, stack=p0)
        be2row = sb("be2row", [1, D], F32, stack=p0)
        be1row = sb("be1row", [1, D], F32, stack=p0)
        rows = sb("rows", [1, 4 * D], F32, stack=p0)
        OFF_BK, OFF_BV, OFF_BO, OFF_B2 = 0, D, 2 * D, 3 * D
        for dst, srcap in [(c_id[:], identf[:, :]), (c_bc2[:], dbc2[:, :]),
                           (sc_clip[:], dclip[:, :]),
                           (c_sel[:], dsel[:, :]),
                           (g1row[:], dg1[None, :]), (g2row[:], dg2[None, :]),
                           (be2row[:], dbe2[None, :]), (be1row[:], dbe1[None, :])]:
            E.dma("sync", dst, srcap, sMisc, serial=True)
        colstg = sb("colstg", [DC, 128], F32, stack=p0)
        pcols0 = psum("pcols0", [128, 512], F32, stack=p0)
        e_cid = E.stream_ev(sMisc)
        e_colprev = None
        for dsrc, dstc in [(dg1, g1cols), (dbe1, be1cols), (dbq, bq_cols)]:
            e_cl = E.dma("sync", colstg[:], dsrc.rearrange("(c p) -> c p", p=128),
                         sMisc, after=[e_colprev], serial=True)
            e_ct = E.op("tensor", lambda t: t.transpose(pcols0[:, 0:DC], colstg[:], c_id[0:DC, 0:DC]),
                        after=[e_cl, e_cid, e_colprev], inc=CT)
            e_colprev = E.op("vector", lambda v, d=dstc: v.tensor_copy(d[:], pcols0[:, 0:DC]),
                             after=[e_ct], inc=CV)
        for srcd, off, n in [(dbk, OFF_BK, D), (dbv, OFF_BV, D), (dbo, OFF_BO, D),
                             (db2, OFF_B2, D)]:
            E.dma("sync", rows[0:1, off:off + n], srcd[None, :], sMisc, serial=True)
        ldall = E.stream_ev(sMisc)

        e_idbf = E.op("vector", lambda v: v.tensor_copy(c_id_bf[:], c_id[:]),
                      after=[ldall], inc=CV)
        e_ms = E.op("vector", lambda v: (
            v.memset(amax_sl[:], 0.0),
            v.memset(c_magic[:], MAGIC),
            v.memset(c_nmagic[:], -MAGIC),
            v.memset(c_ones_bf[:], 1.0),
            v.memset(c_ones_f[:], 1.0),
            v.memset(zlhs[:], 0.0),
            v.memset(kvbd[:], 0.0),
            v.memset(sc_tmp[:], 0.0))[-1], inc=CV)
        e_rows = E.op("vector", lambda v: (
            v.tensor_copy(r_bk[:], rows[0:1, OFF_BK:OFF_BK + D]),
            v.tensor_copy(r_bv[:], rows[0:1, OFF_BV:OFF_BV + D]),
            v.tensor_copy(r_bo[:], rows[0:1, OFF_BO:OFF_BO + D]),
            v.tensor_add(rows[0:1, OFF_B2:OFF_B2 + D], rows[0:1, OFF_B2:OFF_B2 + D],
                         be1row[0:1, :]),
            v.drain(),
            v.tensor_copy(r_b2pp[:], rows[0:1, OFF_B2:OFF_B2 + D]))[-1],
            after=[ldall], inc=CV)
        e_sig = E.op("scalar", lambda s: s.activation(sc_clip[:], sc_clip[:], AF.Sigmoid),
                     after=[ldall], inc=CS)

        pbc = psum("pbc", [128, 512], F32, stack=p0)
        prev_evac = None
        for dst, rowap in [(g1b, g1row), (g2b, g2row), (be2b, be2row)]:
            for half in range(2):
                e_mm = E.op("tensor", lambda t, r=rowap, h=half: t.matmul(
                    pbc[:, :], c_ones_f[0:1, 0:128], r[0:1, h * 512:(h + 1) * 512],
                    start=True, stop=True), after=[ldall, prev_evac, e_ms], inc=CT)
                prev_evac = E.op("vector", lambda v, d=dst, h=half: v.tensor_copy(
                    d[:, h * 512:(h + 1) * 512], pbc[:, :]), after=[e_mm], inc=CV)

        wst = [sb(f"wst{i}", [128, D], F32, stack=p0) for i in range(2)]
        prev_cast = [None, None]
        for wi, (dsrc, dst) in enumerate([(dwk, wk_bf), (dwv, wv_bf)]):
            for icx in range(DC):
                idx = wi * DC + icx
                par = idx % 2
                e_l = E.dma("sync" if par == 0 else "scalar",
                            wst[par][:], dsrc[icx * 128:(icx + 1) * 128, :],
                            sLb2[par], after=[prev_cast[par]])
                prev_cast[par] = E.op("vector", lambda v, d=dst, i=icx, p=par: v.tensor_copy(
                    d[:, i * D:(i + 1) * D], wst[p][:]), after=[e_l], inc=CV)

        E.barrier(GATE, {"vector": [prev_cast[0], prev_cast[1], e_ms, e_rows, prev_evac],
                         "scalar": [e_sig]})
        p0.close()

        TS1 = 256
        NT1 = S // TS1
        CPT1 = TS1 // 128
        p1 = contextlib.ExitStack()
        xbuf = [sb(f"xbuf{i}", [128, CPT1 * D], F32, stack=p1) for i in range(2)]
        xnf_t = [sb(f"xnf{i}", [128, CPT1 * D], F32, stack=p1) for i in range(3)]
        xnfm = [sb(f"xnfm{i}", [128, DC * TS1], BF16, stack=p1) for i in range(2)]
        stt = sb("stt", [128, 12], F32, stack=p1)
        agg = [sb(f"agg{i}", [128, CPT1 * 2], F32, stack=p1) for i in range(2)]
        rstd_t = [sb(f"rstd_t{i}", [128, CPT1], F32, stack=p1) for i in range(2)]
        negmr_t = [sb(f"negmr_t{i}", [128, CPT1], F32, stack=p1) for i in range(2)]
        kb = [sb(f"kb{i}", [128, D], BF16, stack=p1) for i in range(2)]
        vaug = [sb(f"vaug{i}", [128, H * 65], BF16, stack=p1) for i in range(2)]
        ka_bf = sb("ka_bf", [128, D], BF16, stack=p1)
        ke_bf = sb("ke_bf", [128, D], BF16, stack=p1)
        wst2 = [sb(f"wst2_{i}", [128, D], F32, stack=p1) for i in range(2)]
        e_qkvo = [None, None]

        def wqo_steps():
            prev_c = [None, None]
            for wi, (dsrc, dst) in enumerate([(dwq, wq_bf), (dwo, wo_bf)]):
                for icx in range(DC):
                    idx = wi * DC + icx
                    par = idx % 2
                    e_l = E.dma("sync" if par == 0 else "scalar",
                                wst2[par][:], dsrc[icx * 128:(icx + 1) * 128, :],
                                sLc[par], after=[prev_c[par]])
                    prev_c[par] = E.op("scalar", lambda s, d=dst, i=icx, p=par: s.activation(
                        d[:, i * D:(i + 1) * D], wst2[p][:], AF.Identity),
                        after=[e_l], inc=CS)
                    e_qkvo[par] = prev_c[par]
                    yield
            while True:
                yield
        wqogen = wqo_steps()

        wqst = [sb(f"wqst{i}", [128, D], F32, stack=p1) for i in range(3)]
        wqout = [sb(f"wqout{i}", [128, D], BF16, stack=p1) for i in range(3)]
        wqsc = sb("wqsc", [128, 6], F32, stack=p1)
        pk = psum("pk", [128, 1024], F32, stack=p1)
        pv = psum("pv", [128, 1024], F32, stack=p1)
        kvA = psum("kvA", [128, 512], F32, stack=p1)
        kvB = psum("kvB", [128, 512], F32, stack=p1)
        ptr1 = psum("ptr1", [128, 1024], F32, stack=p1)

        e_vone = E.op("vector", lambda v: (
            v.memset(vaug[0][:], 1.0), v.memset(vaug[1][:], 1.0))[-1], inc=CV)

        def wq_steps():
            chunks = [(dsrc, hdst, ci) for dsrc, hdst in [(dwg, hwgq), (dwu, hwuq)]
                      for ci in range(ICC)]
            NQ = len(chunks)
            prev_q = [None, None, None]
            prev_mul = [None, None, None]
            e_ld = {}
            for k in range(NQ + 1):
                if k < NQ:
                    dsrc, hdst, ci = chunks[k]
                    par = k % 3
                    e_ld[k] = E.dma("gpsimd", wqst[par][:], dsrc[ci * 128:(ci + 1) * 128, :],
                                    sW[par], after=[prev_mul[par]])
                if k >= 1:
                    dsrc, hdst, ci = chunks[k - 1]
                    par = (k - 1) % 3

                    def qred(v, par=par):
                        v.drain()
                        v.tensor_reduce(wqsc[:, 2 * par:2 * par + 1], wqst[par][:], AX.X, ALU.max,
                                        apply_absolute_value=True)
                        v.drain()
                        v.tensor_scalar_mul(wqsc[:, 2 * par:2 * par + 1],
                                            wqsc[:, 2 * par:2 * par + 1], 1.0 / 127.0)
                        v.drain()
                        return v.reciprocal(wqsc[:, 2 * par + 1:2 * par + 2],
                                            wqsc[:, 2 * par:2 * par + 1])
                    e_r1 = E.op("vector", qred, after=[e_ld[k - 1], prev_mul[par]], inc=CV)
                    yield
                    def qfin(v, par=par):
                        v.drain()
                        v.tensor_scalar(wqst[par][:], wqst[par][:], wqsc[:, 2 * par + 1:2 * par + 2],
                                        MAGIC, ALU.mult, ALU.add)
                        v.drain()
                        return v.tensor_scalar(wqout[par][:], wqst[par][:], MAGIC,
                                               wqsc[:, 2 * par:2 * par + 1],
                                               ALU.subtract, ALU.mult)
                    e_mul = E.op("vector", qfin, after=[e_r1, prev_q[par]], inc=CV)
                    prev_mul[par] = e_mul
                    prev_q[par] = E.dma("gpsimd", hdst[ci * 128:(ci + 1) * 128, :],
                                        wqout[par][:], sWs[par], after=[e_mul])
                yield
            while True:
                yield
        wqgen = wq_steps()

        e_xload = [None, None]
        e_sq_pp = [None, None]
        e_fs_pp = [None, None]
        e_am_pp = [None, None]
        e_xnf = [None, None]
        e_xnb = [None, None]
        e_spf = [None, None]
        e_spb = [None, None]
        e_fmload = [None, None]
        e_tr1e_half = [None, None]
        kv_pend = [None]
        e_tr1ev_t = {}
        e_spf3 = [None, None, None]
        e_apply_c = {}
        e_am3 = [None, None, None]
        e_trp3 = [None, None, None]
        e_apply = {}
        e_sttH = {}
        e_vcH = {}
        e_kvmmh = {}
        e_kfin = {}
        e_vcopy = {}
        e_kvmm = {}
        e_proj_tile = {}

        def emit_stageA(j):
            par = j % 2
            t0 = j * TS1
            e_xload[par] = E.dma("sync", xbuf[par][:], tm_view(dx, t0, TS1), sL[par],
                                 after=[e_xnf[par]])

            def stats(v, par=par):
                v.drain()
                for c in range(CPT1):
                    v.bn_stats(stt[:, 0:6], xbuf[par][:, c * D:c * D + 512])
                    v.bn_stats(stt[:, 6:12], xbuf[par][:, c * D + 512:c * D + 1024])
                    v.drain()
                    v.bn_aggr(agg[par][:, 2 * c:2 * c + 2], stt[:])
                    v.drain()
                v.tensor_scalar_add(
                    rstd_t[par][:], agg[par][:].rearrange("p (c two) -> p c two", two=2)[:, :, 1], LN_EPS)
                v.drain()
                return v.reciprocal(rstd_t[par][:], rstd_t[par][:])
            e_st = E.op("vector", stats, after=[e_xload[par], e_fs_pp[par]], inc=CV)
            e_sq_pp[par] = e_st
            next(wqogen)
            next(wqgen)
            next(wqgen)

        def emit_stageB1(j):
            par = j % 2
            p3 = j % 3
            t0 = j * TS1

            def pre_ln(s, par=par):
                s.activation(rstd_t[par][:], rstd_t[par][:], AF.Sqrt)
                s.drain()
                m_v = agg[par][:].rearrange("p (c two) -> p c two", two=2)
                for c in range(CPT1):
                    s.activation(negmr_t[par][:, c:c + 1], m_v[:, c, 0:1], AF.Identity,
                                 scale=rstd_t[par][:, c:c + 1])
                s.drain()
                return s.activation(negmr_t[par][:], negmr_t[par][:], AF.Identity,
                                    scale=-1.0)
            e_pre = E.op("scalar", pre_ln, after=[e_sq_pp[par], e_fs_pp[par]], inc=CS)

            e_ap = None
            for c in range(CPT1):
                def ap1(s, par=par, p3=p3, c=c):
                    return s.activation(xnf_t[p3][:, c * D:(c + 1) * D],
                                        xbuf[par][:, c * D:(c + 1) * D], AF.Identity,
                                        bias=negmr_t[par][:, c:c + 1],
                                        scale=rstd_t[par][:, c:c + 1])
                e_ap = E.op("scalar", ap1,
                            after=[e_pre, e_spf3[p3] if c == 0 else None,
                                   e_am3[p3] if c == 0 else None,
                                   e_trp3[p3] if c == 0 else None], inc=CS)
                e_apply_c[(j, c)] = e_ap
            e_xnf[par] = e_ap
            e_apply[j] = e_ap
            e_fs_pp[par] = e_ap

            e_spf3[p3] = E.dma("scalar", tm_view(hxnf, t0, TS1), xnf_t[p3][:],
                               sSp[p3], after=[e_ap])
            next(wqgen)
            next(wqgen)

        def emit_kvmm_h(gc, cpar, half):
            def kvmm(t, cp=cpar, half=half, first=(gc == 0), last_=(gc == NCH - 1)):
                out = None
                ptile = kvA if half == 0 else kvB
                for hh in range(8):
                    h = 8 * half + hh
                    g = hh // 2
                    rowb = 64 * (hh % 2)
                    out = t.matmul(ptile[rowb:rowb + 64, g * 65:(g + 1) * 65],
                                   kb[cp][:, h * 64:(h + 1) * 64],
                                   vaug[cp][:].rearrange("p (h w) -> p h w", w=65)[:, h, :],
                                   start=(first and hh in (0, 1)),
                                   stop=(last_ and hh in (6, 7)),
                                   tile_position=(0, rowb),
                                   skip_group_check=True)
                return out
            e_kvmmh[(gc, half)] = E.op("tensor", kvmm,
                                       after=[e_sttH[(gc, half)], e_vcH[(gc, half)]], inc=CT)

        def emit_stageB2a(j):
            par = j % 2
            p3 = j % 3
            t0 = j * TS1
            for c in range(CPT1):
                e_tr1e = None
                for rnd in range(2):
                    half = rnd
                    def trp1(t, p3=p3, c=c, rnd=rnd, half=half):
                        last = None
                        for k in range(4):
                            icx = rnd * 4 + k
                            last = t.transpose(ptr1[:, half * 512 + k * 128: half * 512 + (k + 1) * 128],
                                               xnf_t[p3][:, c * D + icx * 128: c * D + (icx + 1) * 128],
                                               c_id[:])
                        return last
                    e_trm1 = E.op("tensor", trp1, after=[e_apply_c[(j, c)], e_tr1e_half[half], e_spb[par],
                                                         e_proj_tile.get(j - 2) if (c == 0 and rnd == 0) else None], inc=CT)
                    def trev1(s, par=par, c=c, rnd=rnd, half=half):
                        return s.activation(
                            xnfm[par][:].rearrange("p (i t) -> p i t", t=TS1)[:, rnd * 4:(rnd + 1) * 4,
                                                                              c * 128:(c + 1) * 128],
                            ptr1[:, half * 512:(half + 1) * 512].rearrange("p (i w) -> p i w", w=128),
                            AF.Identity)
                    e_tr1e = E.op("scalar", trev1, after=[e_trm1], inc=CS)
                    e_tr1e_half[half] = e_tr1e
                    e_trp3[p3] = e_trm1
                e_tr1ev_t[(j, c)] = e_tr1e
            e_fmload[par] = e_tr1e
            e_spb[par] = E.dma("scalar",
                               hxnb.rearrange("(c p) s -> p c s", p=128)[:, :, t0:t0 + TS1],
                               xnfm[par][:].rearrange("p (i t) -> p i t", t=TS1),
                               sSp2[par], after=[e_tr1e])
            def absmax(v, p3=p3, j=j):
                return v.tensor_reduce(amax_sl[:, j:j + 1], xnf_t[p3][:], AX.X, ALU.max,
                                       apply_absolute_value=True)
            e_am3[p3] = E.op("vector", absmax, after=[e_apply[j]], inc=CV)

        def emit_stageB2b(j):
            par = j % 2
            e_tr1ev = {c: e_tr1ev_t[(j, c)] for c in range(CPT1)}
            for c in range(CPT1):
                gc = j * CPT1 + c
                cpar = gc % 2

                if kv_pend[0] is not None:
                    pgc, pcp = kv_pend[0]
                    emit_kvmm_h(pgc, pcp, 0)
                    emit_kvmm_h(pgc, pcp, 1)
                    kv_pend[0] = None

                for half in range(2):
                    def kproj(t, par=par, c=c, half=half):
                        for icx in range(DC):
                            lhs = xnfm[par][:, icx * TS1 + c * 128: icx * TS1 + (c + 1) * 128]
                            t.matmul(pk[:, half * 512:(half + 1) * 512], lhs,
                                     wk_bf[:, icx * D + half * 512: icx * D + (half + 1) * 512],
                                     start=(icx == 0), stop=False)
                        return t.matmul(pk[:, half * 512:(half + 1) * 512], c_ones_bf[0:1, 0:128],
                                        r_bk[0:1, half * 512:(half + 1) * 512], start=False, stop=True)
                    e_k = E.op("tensor", kproj,
                               after=[e_tr1ev[c], e_sttH.get((gc - 1, half)), e_vone], inc=CT)

                    def elu_s(s, half=half):
                        s.activation(ke_bf[:, half * 512:(half + 1) * 512],
                                     pk[:, half * 512:(half + 1) * 512], AF.Exp)
                        return s.activation(ka_bf[:, half * 512:(half + 1) * 512],
                                            pk[:, half * 512:(half + 1) * 512], AF.Relu)
                    e_el = E.op("scalar", elu_s,
                                after=[e_k, e_sttH.get((gc - 1, half))], inc=CS)
                    e_sttH[(gc, half)] = E.op("vector", lambda v, cp=cpar, half=half: v.scalar_tensor_tensor(
                        kb[cp][:, half * 512:(half + 1) * 512],
                        ke_bf[:, half * 512:(half + 1) * 512], 1.0,
                        ka_bf[:, half * 512:(half + 1) * 512], ALU.min, ALU.add),
                        after=[e_el, e_kvmmh.get((gc - 2, half))], inc=CV)

                for half in range(2):
                    def vproj(t, par=par, c=c, half=half):
                        for icx in range(DC):
                            lhs = xnfm[par][:, icx * TS1 + c * 128: icx * TS1 + (c + 1) * 128]
                            t.matmul(pv[:, half * 512:(half + 1) * 512], lhs,
                                     wv_bf[:, icx * D + half * 512: icx * D + (half + 1) * 512],
                                     start=(icx == 0), stop=False)
                        return t.matmul(pv[:, half * 512:(half + 1) * 512], c_ones_bf[0:1, 0:128],
                                        r_bv[0:1, half * 512:(half + 1) * 512], start=False, stop=True)
                    e_v = E.op("tensor", vproj,
                               after=[e_tr1ev[c], e_vcH.get((gc - 1, half)), e_vone], inc=CT)
                    e_proj_tile[j] = e_v
                    e_vcH[(gc, half)] = E.op("vector", lambda v, cp=cpar, half=half: v.tensor_copy(
                        vaug[cp][:].rearrange("p (h w) -> p h w", w=65)[:, 8 * half:8 * (half + 1), 0:64],
                        pv[:, half * 512:(half + 1) * 512].rearrange("p (hh w) -> p hh w", w=64)),
                        after=[e_v, e_kvmmh.get((gc - 2, half))], inc=CV)
                kv_pend[0] = (gc, cpar)

            for _ in range(max(1, -(-6 * ICC // NT1) - 4)):
                next(wqgen)

        for j in range(NT1 + 2):
            if 1 <= j <= NT1:
                emit_stageB1(j - 1)
            if j >= 2:
                emit_stageB2a(j - 2)
            if j < NT1:
                emit_stageA(j)
            if j >= 2:
                emit_stageB2b(j - 2)
        pgc, pcp = kv_pend[0]
        emit_kvmm_h(pgc, pcp, 0)
        emit_kvmm_h(pgc, pcp, 1)
        kv_pend[0] = None
        for _ in range(4 * ICC):
            next(wqgen)
        for _ in range(2 * DC):
            next(wqogen)

        def kvevac(v):
            v.drain()
            last = None
            for h in range(H):
                ptile = kvA if h < 8 else kvB
                g = (h % 8) // 2
                rowb = 64 * (h % 2)
                oc = h // 2
                v.tensor_copy(kvbd[rowb:rowb + 64, oc * 128 + rowb: oc * 128 + rowb + 64],
                              ptile[rowb:rowb + 64, g * 65: g * 65 + 64])
                last = v.tensor_copy(zlhs[rowb:rowb + 64, (h // 2) * 16 + h:(h // 2) * 16 + h + 1],
                                     ptile[rowb:rowb + 64, g * 65 + 64: g * 65 + 65])
            return last
        e_kvbd = E.op("vector", kvevac, after=[e_kvmmh[(NCH - 1, 0)], e_kvmmh[(NCH - 1, 1)]], inc=CV)

        E.barrier(GATE, {
            "vector": [e_kvbd],
            "scalar": [e_xnf[0], e_xnf[1], E.stream_ev(sSp[0]), E.stream_ev(sSp[1]),
                       E.stream_ev(sSp[2]), E.stream_ev(sSp2[0]), E.stream_ev(sSp2[1])],
            "tensor": [e_kvmmh[(NCH - 1, 0)], e_kvmmh[(NCH - 1, 1)]],
            "gpsimd": [E.stream_ev(sWs[0]), E.stream_ev(sWs[1]), E.stream_ev(sWs[2])],
            "sync": [],
        })
        p1.close()

        def gpamax(g):
            return g.tensor_reduce(sc_tmp[0:1, 0:1], amax_sl[:], AX.XYZWC, ALU.max)
        e_gmax = E.op("gpsimd", gpamax, after=[e_am3[0], e_am3[1], e_am3[2]], inc=CC)
        e_cin = E.dma("gpsimd", ccx_in[:, :], sc_tmp[0:1, 0:16], sCc, after=[e_gmax], serial=True)
        e_cc1 = E.op("gpsimd", lambda g: g.collective_compute(
            "AllReduce", ALU.max, replica_groups=[list(range(n_cores))],
            ins=[ccx_in[:, :]], outs=[ccx_out[:, :]]), after=[e_cin], inc=CC)
        e_cback = E.dma("gpsimd", sc_tmp[0:1, 1:2], ccx_out[0:1, 0:1], sCc, after=[e_cc1], serial=True)


        p2 = contextlib.ExitStack()
        xnfm2 = [sb(f"xnfm2_{i}", [128, DC * TS], BF16, stack=p2) for i in range(2)]
        xnf2 = [sb(f"xnf2_{i}", [128, D], F32, stack=p2) for i in range(2)]
        qb = sb("qb", [128, DC * TS], BF16, stack=p2)
        qsb = sb("qsb", [128, DC * TS], BF16, stack=p2)
        ob = sb("ob", [128, DC * TS], BF16, stack=p2)
        qa_bf = sb("qa_bf", [128, TS], BF16, stack=p2)
        qe_bf = sb("qe_bf", [128, TS], BF16, stack=p2)
        zf = sb("zf", [16, TS], F32, stack=p2)
        zbf = sb("zbf", [16, TS], BF16, stack=p2)
        ub = sb("ub", [128, D], F32, stack=p2)
        lnb_t = [sb(f"lnb{i}", [128, CPT * D], BF16, stack=p2) for i in range(2)]
        stt2 = sb("stt2", [128, 12], F32, stack=p2)
        agg2 = sb("agg2", [128, CPT * 2], F32, stack=p2)
        rstd2 = sb("rstd2", [128, CPT], F32, stack=p2)
        negmr2 = sb("negmr2", [128, CPT], F32, stack=p2)
        pq = psum("pq", [128, 2 * 512], F32, stack=p2)
        pz = psum("pz", [16, 512], F32, stack=p2)
        pzb = psum("pzb", [128, 512], F32, stack=p2)
        po = psum("po", [128, 2 * 512], F32, stack=p2)
        pat = psum("pat", [128, 1024], F32, stack=p2)


        e_l2a = [None, None]
        e_lnc = [None, None]
        e_sp2 = [None, None]
        e_qev = [None, None]
        e_oev = [None, None]
        e_uadd = None
        e_uadd_pp = [None, None]
        e_zf = None
        e_zb_last = None
        e_lnc_last = None
        e_qfin_tile = {}
        e_o_tile = {}
        e_at_tile = {}
        e_qmm_tile = {}

        for j in range(NT):
            par = j % 2
            t0 = j * TS
            e_l2a[par] = E.dma("sync",
                               xnfm2[par][:].rearrange("p (i t) -> p i t", t=TS),
                               hxnb.rearrange("(c p) s -> p c s", p=128)[:, :, t0:t0 + TS],
                               sLb[par], after=[e_qmm_tile.get(j - 2)])

            prev_qs = e_qfin_tile.get(j - 1, [None] * DC)
            prev_o = e_o_tile.get(j - 1, [None] * DC)
            qs_events = [None] * DC
            o_events = [None] * DC
            for oc in range(DC):
                qpar = oc % 2

                def qproj(t, par=par, oc=oc, qpar=qpar):
                    last = None
                    for icx in range(DC):
                        last = t.matmul(pq[:, qpar * 512:(qpar + 1) * 512],
                                        wq_bf[:, icx * D + oc * 128: icx * D + oc * 128 + 128],
                                        xnfm2[par][:, icx * TS:(icx + 1) * TS],
                                        start=(icx == 0), stop=(icx == DC - 1))
                    return last
                e_q = E.op("tensor", qproj, after=[e_l2a[par], e_qev[qpar], e_qkvo[0], e_qkvo[1]], inc=CT)
                e_qmm_tile[j] = e_q

                e_m = E.op("vector", lambda v, oc=oc, qp=qpar: v.tensor_scalar(
                    qa_bf[:], pq[:, qp * 512:(qp + 1) * 512],
                    bq_cols[:, oc:oc + 1], 0.0, ALU.add, ALU.min),
                    after=[e_q, e_qev[0], e_qev[1]], inc=CV)
                e_e = E.op("scalar", lambda s: s.activation(qe_bf[:], qa_bf[:], AF.Exp),
                           after=[e_m], inc=CS)

                def qfin(v, oc=oc, qp=qpar):
                    v.drain()
                    v.tensor_scalar(qa_bf[:], pq[:, qp * 512:(qp + 1) * 512],
                                    bq_cols[:, oc:oc + 1], 0.0, ALU.add, ALU.max)
                    v.drain()
                    return v.tensor_add(qb[:, oc * TS:(oc + 1) * TS], qa_bf[:], qe_bf[:])
                e_qev[qpar] = E.op("vector", qfin, after=[e_e, prev_qs[oc]], inc=CV)
                qs_events[oc] = e_qev[qpar]

            e_z = None
            for oc in range(DC):
                e_z = E.op("tensor", lambda t, oc=oc: t.matmul(
                    pz[:], zlhs[:, oc * 16:(oc + 1) * 16], qb[:, oc * TS:(oc + 1) * TS],
                    start=(oc == 0), stop=(oc == DC - 1)),
                    after=[qs_events[oc], e_zf], inc=CT)

            def zcalc(v):
                v.drain()
                v.tensor_scalar_add(zf[:], pz[:], Z_EPS)
                v.drain()
                v.reciprocal(zf[:], zf[:])
                v.drain()
                return v.tensor_copy(zbf[:], zf[:])
            e_zf = E.op("vector", zcalc, after=[e_z, e_zb_last], inc=CV)

            e_zbev = None
            for oc in range(DC):
                opar = oc % 2
                e_zb = E.op("tensor", lambda t, oc=oc: t.matmul(
                    pzb[:, :], c_sel[:, oc * 128:(oc + 1) * 128], zbf[:, :],
                    start=True, stop=True),
                    after=[e_zf, e_zbev], inc=CT)
                e_zb_last = e_zb

                def qsmul(v, oc=oc):
                    return v.tensor_tensor(qsb[:, oc * TS:(oc + 1) * TS],
                                           qb[:, oc * TS:(oc + 1) * TS], pzb[:, :], ALU.mult)
                e_zbev = E.op("vector", qsmul, after=[e_zb, prev_o[oc]], inc=CV)
                qs_events[oc] = e_zbev

                e_o = E.op("tensor", lambda t, oc=oc, op=opar: t.matmul(
                    po[:, op * 512:(op + 1) * 512], kvbd[:, oc * 128:(oc + 1) * 128],
                    qsb[:, oc * TS:(oc + 1) * TS], start=True, stop=True),
                    after=[e_zbev, e_oev[opar]], inc=CT)
                o_events[oc] = e_o
                e_oev[opar] = E.op("vector", lambda v, oc=oc, op=opar: v.tensor_copy(
                    ob[:, oc * TS:(oc + 1) * TS], po[:, op * 512:(op + 1) * 512]),
                    after=[e_o, e_at_tile.get(j - 1)], inc=CV)
            e_qfin_tile[j] = qs_events
            e_o_tile[j] = o_events

            for c in range(CPT):
                def atmm(t, c=c):
                    last = None
                    for icx in range(DC):
                        lhs = ob[:, icx * TS + c * 128: icx * TS + (c + 1) * 128]
                        for half in range(2):
                            t.matmul(pat[:, half * 512:(half + 1) * 512], lhs,
                                     wo_bf[:, icx * D + half * 512: icx * D + (half + 1) * 512],
                                     start=(icx == 0), stop=False)
                    for half in range(2):
                        last = t.matmul(pat[:, half * 512:(half + 1) * 512], c_ones_bf[0:1, 0:128],
                                        r_bo[0:1, half * 512:(half + 1) * 512], start=False, stop=True)
                    return last
                e_at = E.op("tensor", atmm, after=[e_oev[0], e_oev[1], e_uadd, e_qkvo[0], e_qkvo[1]], inc=CT)
                e_at_tile[j] = e_at

                cpar2 = (j * CPT + c) % 2
                e_xn2 = E.dma("sync", xnf2[cpar2][:],
                              tm_view(hxnf, t0 + c * 128, 128), sL[cpar2],
                              after=[e_uadd_pp[cpar2]])
                e_uadd = E.op("vector", lambda v, cp=cpar2: v.tensor_add(
                    ub[:], pat[:], xnf2[cp][:]),
                    after=[e_at, e_xn2, e_lnc_last], inc=CV)
                e_uadd_pp[cpar2] = e_uadd

                def stats2(v, c=c):
                    v.drain()
                    v.bn_stats(stt2[:, 0:6], ub[:, 0:512])
                    v.bn_stats(stt2[:, 6:12], ub[:, 512:1024])
                    v.drain()
                    v.bn_aggr(agg2[:, 2 * c:2 * c + 2], stt2[:])
                    v.drain()
                    return v.tensor_scalar_add(rstd2[:, c:c + 1],
                                               agg2[:, 2 * c + 1:2 * c + 2], LN_EPS)
                e_s2 = E.op("vector", stats2, after=[e_uadd], inc=CV)
                e_sq2 = E.op("scalar", lambda s, c=c: s.activation(
                    rstd2[:, c:c + 1], rstd2[:, c:c + 1], AF.Sqrt), after=[e_s2], inc=CS)

                def fin2(v, c=c):
                    v.reciprocal(rstd2[:, c:c + 1], rstd2[:, c:c + 1])
                    v.drain()
                    v.tensor_tensor(negmr2[:, c:c + 1], agg2[:, 2 * c:2 * c + 1],
                                    rstd2[:, c:c + 1], ALU.mult)
                    v.drain()
                    return v.tensor_scalar_mul(negmr2[:, c:c + 1], negmr2[:, c:c + 1], -1.0)
                e_f2 = E.op("vector", fin2, after=[e_sq2], inc=CV)
                e_lnc[par] = E.op("scalar", lambda s, par=par, c=c: s.activation(
                    lnb_t[par][:, c * D:(c + 1) * D], ub[:], AF.Identity,
                    bias=negmr2[:, c:c + 1], scale=rstd2[:, c:c + 1]),
                    after=[e_f2, e_sp2[par] if c == 0 else None], inc=CS)
                e_lnc_last = e_lnc[par]
            e_sp2[par] = E.dma("scalar", tm_view(hlnc1, t0, TS), lnb_t[par][:],
                               sSp[par], after=[e_lnc[par]])

        E.barrier(GATE, {
            "vector": [e_zf, e_oev[0], e_oev[1], e_uadd],
            "scalar": [e_lnc[0], e_lnc[1], E.stream_ev(sSp[0]), E.stream_ev(sSp[1])],
            "tensor": [e_at_tile[NT - 1]],
            "gpsimd": [E.stream_ev(sWs[0]), E.stream_ev(sWs[1])],
            "sync": [],
        })
        p2.close()
        s_qkvo.close()

        p3 = contextlib.ExitStack()
        xnf3 = [sb(f"xnf3_{i}", [128, CPT * D], F32, stack=p3) for i in range(2)]
        rb_t = [sb(f"rb{i}", [128, CPT * D], BF16, stack=p3) for i in range(2)]
        rfm = [sb(f"rfm{i}", [128, DC * TS], BF16, stack=p3) for i in range(2)]
        rt1 = sb("rt1", [128, D], F32, stack=p3)
        rt2 = sb("rt2", [128, D], F32, stack=p3)
        wgqT = sb("wgqT", [128, DC * I_FF], BF16, stack=p3)
        wuqT = sb("wuqT", [128, DC * I_FF], BF16, stack=p3)
        sgb = [sb(f"sgb{i}", [128, 512], BF16, stack=p3) for i in range(2)]
        pbuf = [sb(f"pbuf{i}", [128, 512], F32, stack=p3) for i in range(2)]
        pg = psum("pg", [128, 2 * 512], F32, stack=p3)
        pu = psum("pu", [128, 2 * 512], F32, stack=p3)
        ptr3 = psum("ptr3", [128, 1024], BF16, stack=p3)
        pcol3 = psum("pcol3", [1, 512], F32, stack=p3)
        pbx3 = psum("pbx3", [128, 512], F32, stack=p3)

        IHH = I_FF // 2
        wdst = sb("wdst", [128, I_FF], F32, stack=p3)
        wdout = sb("wdout", [128, IHH], BF16, stack=p3)
        wdsc = sb("wdsc", [128, 2], F32, stack=p3)

        def wd_steps():
            e_ld = E.dma("gpsimd", wdst[:], dwd[0:128, :], sW[0])
            yield
            prev_st = None
            for ci in range(DC):
                def quant(v):
                    v.drain()
                    v.tensor_reduce(wdsc[:, 0:1], wdst[:], AX.X, ALU.max,
                                    apply_absolute_value=True)
                    v.drain()
                    v.tensor_scalar_mul(wdsc[:, 0:1], wdsc[:, 0:1], 1.0 / 127.0)
                    v.drain()
                    v.reciprocal(wdsc[:, 1:2], wdsc[:, 0:1])
                    v.drain()
                    return v.tensor_scalar(wdst[:], wdst[:], wdsc[:, 1:2],
                                           MAGIC, ALU.mult, ALU.add)
                e_q = E.op("vector", quant, after=[e_ld], inc=CV)
                yield
                e_t2 = None
                for h in range(2):
                    e_t2 = E.op("vector", lambda v, h=h: v.tensor_scalar(
                        wdout[:], wdst[:, h * IHH:(h + 1) * IHH], MAGIC, wdsc[:, 0:1],
                        ALU.subtract, ALU.mult), after=[e_q, prev_st], inc=CV)
                    prev_st = E.dma("gpsimd", hwdq[ci * 128:(ci + 1) * 128, h * IHH:(h + 1) * IHH],
                                    wdout[:], sWs[ci % 2], after=[e_t2])
                if ci + 1 < DC:
                    e_ld = E.dma("gpsimd", wdst[:], dwd[(ci + 1) * 128:(ci + 2) * 128, :],
                                 sW[(ci + 1) % 2], after=[e_t2])
                yield
            while True:
                yield
        wdgen = wd_steps()


        def sxcalc(v):
            v.drain()
            v.tensor_tensor(sc_tmp[0:1, 2:3], sc_tmp[0:1, 1:2], sc_clip[0:1, 0:1], ALU.mult)
            v.drain()
            v.tensor_scalar_mul(sc_tmp[0:1, 2:3], sc_tmp[0:1, 2:3], 1.0 / 127.0)
            v.drain()
            return v.reciprocal(sc_tmp[0:1, 3:4], sc_tmp[0:1, 2:3])
        e_sxg = E.op("vector", sxcalc, after=[e_cback, e_sig], inc=CV)

        e_bxx = E.op("tensor", lambda t: t.matmul(pbx3[:, 0:2], c_ones_f[0:1, 0:128],
                                                  sc_tmp[0:1, 2:4], start=True, stop=True),
                     after=[e_sxg], inc=CT)

        def sxevac(v):
            v.tensor_copy(sc_sx[:], pbx3[:, 0:1])
            return v.tensor_copy(sc_rsx[:], pbx3[:, 1:2])
        e_sxb = E.op("vector", sxevac, after=[e_bxx], inc=CV)

        wq_sp_done0 = E.stream_ev(sWs[0])
        wq_sp_done1 = E.stream_ev(sWs[1])
        wq_sp_done2 = E.stream_ev(sWs[2])
        IH = 1408
        e_wT_half = [None, None]
        for half in range(2):
            for wtile, hsrc in [(wgqT, hwgq), (wuqT, hwuq)]:
                for icx in range(DC):
                    e_wT_half[half] = E.dma(
                        "sync", wtile[:, icx * I_FF + half * IH: icx * I_FF + (half + 1) * IH],
                        hsrc[half * IH:(half + 1) * IH, icx * 128:(icx + 1) * 128],
                        sWt if half == 0 else sWt2,
                        after=[wq_sp_done0, wq_sp_done1, wq_sp_done2], transpose=True)
        e_wTa = e_wT_half[0]
        e_wTb = e_wT_half[1]

        e_l3 = [None, None]
        e_rdone = [None, None]
        e_rsp = [None, None]
        e_rfm = [None, None]
        e_pev = [None, None]
        e_sg = [None, None]
        e_pspill = {}
        e_gu_tile = {}

        for j in range(NT):
            par = j % 2
            t0 = j * TS
            e_l3[par] = E.dma("sync", xnf3[par][:], tm_view(hxnf, t0, TS), sL[par],
                              after=[e_rdone[par]])

            def rcalc(v, par=par):
                last = None
                for c in range(CPT):
                    xin = xnf3[par][:, c * D:(c + 1) * D]
                    v.tensor_scalar(rt1[:], xin, sc_rsx[:, 0:1], MAGIC, ALU.mult, ALU.add)
                    v.drain()
                    v.tensor_scalar(rt2[:], rt1[:], MAGIC, 127.0, ALU.subtract, ALU.min)
                    v.drain()
                    last = v.tensor_scalar_max(rb_t[par][:, c * D:(c + 1) * D], rt2[:], -128.0)
                    v.drain()
                return last
            e_rdone[par] = E.op("vector", rcalc, after=[e_l3[par], e_sxb, e_rfm[par]], inc=CV)
            e_tr3e = None
            for c in range(CPT):
                def trp(t, par=par, c=c):
                    last = None
                    for icx in range(DC):
                        last = t.transpose(ptr3[:, icx * 128:(icx + 1) * 128],
                                           rb_t[par][:, c * D + icx * 128: c * D + (icx + 1) * 128],
                                           c_id_bf[:])
                    return last
                e_trm = E.op("tensor", trp, after=[e_rdone[par], e_tr3e,
                                                  e_gu_tile.get(j - 2) if c == 0 else None], inc=CT)
                def trev(v, par=par, c=c):
                    return v.tensor_copy(
                        rfm[par][:].rearrange("p (i t) -> p i t", t=TS)[:, :, c * 128:(c + 1) * 128],
                        ptr3[:].rearrange("p (i w) -> p i w", w=128))
                e_tr3e = E.op("vector", trev, after=[e_trm], inc=CV)
            e_rfm[par] = e_tr3e

            for oci in range(ICC):
                gpar = oci % 2

                def gumm(t, par, oci, gpar, which):
                    last = None
                    wsrc = wgqT if which == 0 else wuqT
                    pdst = pg if which == 0 else pu
                    for icx in range(DC):
                        last = t.matmul(pdst[:, gpar * 512:(gpar + 1) * 512],
                                        wsrc[:, icx * I_FF + oci * 128: icx * I_FF + oci * 128 + 128],
                                        rfm[par][:, icx * TS:(icx + 1) * TS],
                                        start=(icx == 0), stop=(icx == DC - 1))
                    return last
                e_g = E.op("tensor", lambda t, par=par, oci=oci, gp=gpar: gumm(t, par, oci, gp, 0),
                           after=[e_rfm[par], e_wT_half[0] if oci < IH // 128 else e_wT_half[1],
                                  e_sg[gpar]], inc=CT)
                e_u = E.op("tensor", lambda t, par=par, oci=oci, gp=gpar: gumm(t, par, oci, gp, 1),
                           after=[e_pev[gpar], e_wT_half[0] if oci < IH // 128 else e_wT_half[1]], inc=CT)
                e_gu_tile[j] = e_u
                e_sg[gpar] = E.op("scalar", lambda s, gp=gpar: s.activation(
                    sgb[gp][:], pg[:, gp * 512:(gp + 1) * 512], AF.Silu,
                    scale=sc_sx[:, 0:1]), after=[e_g, e_sxb], inc=CS)

                def pmul(v, gp=gpar, j=j, oci=oci):
                    v.scalar_tensor_tensor(pbuf[gp][:], pu[:, gp * 512:(gp + 1) * 512],
                                           sc_sx[:, 0:1], sgb[gp][:], ALU.mult, ALU.mult)
                    v.drain()
                    return v.tensor_reduce(pmax_sl[:, j * ICC + oci: j * ICC + oci + 1],
                                           pbuf[gp][:], AX.X, ALU.max, apply_absolute_value=True)
                prev_sp = (e_pspill.get((j, oci - 2)) if oci >= 2
                           else e_pspill.get((j - 1, ((ICC - 1 - oci) // 2) * 2 + oci)))
                e_pev[gpar] = E.op("vector", pmul, after=[e_u, e_sg[gpar], prev_sp], inc=CV)
                e_pspill[(j, oci)] = E.dma("scalar", hp[oci * 128:(oci + 1) * 128, t0:t0 + TS],
                                           pbuf[gpar][:], sSp2[gpar], after=[e_pev[gpar]])
            next(wdgen)
            next(wdgen)

        for _ in range(2 * DC + 2):
            next(wdgen)

        if dbg:
            E.dma("gpsimd", hsc[:, :], sc_tmp[0:1, 0:16], sCc, after=[e_pev[0], e_pev[1]], serial=True)
            E.dma("gpsimd", hrfm[:, :], rfm[0][:], sCc, after=[e_gu_tile.get(NT - 1)], serial=True)
        E.barrier(GATE, {
            "vector": [e_pev[0], e_pev[1], e_sxb],
            "scalar": [e_sg[0], e_sg[1], E.stream_ev(sSp2[0]), E.stream_ev(sSp2[1]),
                       E.stream_ev(sSp[0]), E.stream_ev(sSp[1])],
            "tensor": [e_gu_tile[NT - 1], e_bxx],
            "gpsimd": [],
            "sync": [],
        })
        p3.close()

        def gpdmax(g):
            return g.tensor_reduce(sc_tmp[0:1, 4:5], pmax_sl[:], AX.XYZWC, ALU.max)
        e_pmax = E.op("gpsimd", gpdmax, after=[e_pev[0], e_pev[1]], inc=CC)
        e_cin2 = E.dma("gpsimd", ccd_in[:, :], sc_tmp[0:1, 0:16], sCc, after=[e_pmax], serial=True)
        e_cc2 = E.op("gpsimd", lambda g: g.collective_compute(
            "AllReduce", ALU.max, replica_groups=[list(range(n_cores))],
            ins=[ccd_in[:, :]], outs=[ccd_out[:, :]]), after=[e_cin2], inc=CC)
        e_cback2 = E.dma("gpsimd", sc_tmp[0:1, 5:6], ccd_out[0:1, 4:5], sCc, after=[e_cc2], serial=True)


        p4 = contextlib.ExitStack()
        w1b = sb("w1b", [128, DC * FF], BF16, stack=p4)
        w2b = sb("w2b", [128, FFC * D], BF16, stack=p4)
        r_b1p = sb("r_b1p", [1, FF], BF16, stack=p4)
        p4pro = contextlib.ExitStack()
        w4st = [sb(f"w4st{i}", [128, FF], F32, stack=p4pro) for i in range(2)]
        w4st3 = sb("w4st3", [128, D], F32, stack=p4pro)
        b1cols = sb("b1cols", [128, FFC], F32, stack=p4pro)
        b1cols_bf = sb("b1cols_bf", [128, FFC], BF16, stack=p4pro)
        b1stg = sb("b1stg", [FFC, 128], F32, stack=p4pro)
        b1tr = sb("b1tr", [FFC, 128], BF16, stack=p4pro)
        pb1 = psum("pb1", [128, 512], F32, stack=p4pro)
        pb1T = psum("pb1T", [128, 512], F32, stack=p4pro)
        pb1bf = psum("pb1bf", [FFC, 1024], BF16, stack=p4pro)
        e_b1stg = E.dma("sync", b1stg[:], db1.rearrange("(c p) -> c p", p=128), sMisc, serial=True)
        e_b1tr = E.op("tensor", lambda t: t.transpose(pb1T[:, 0:FFC], b1stg[:], c_id[0:FFC, 0:FFC]),
                      after=[e_b1stg], inc=CT)
        e_b1raw = E.op("vector", lambda v: v.tensor_copy(b1cols[:], pb1T[:, 0:FFC]),
                       after=[e_b1tr], inc=CV)

        e_w1c = [None, None, None]
        e_b1mm = None
        for icx in range(DC):
            par = icx % 2
            e_lw1 = E.dma("sync" if par == 0 else "scalar",
                          w4st[par][:], dw1[icx * 128:(icx + 1) * 128, :], sW4[par],
                          after=[e_w1c[par]])

            def b1mms(t, icx=icx, par=par):
                last = None
                for oc in range(FFC):
                    last = t.matmul(pb1[:, oc:oc + 1],
                                    w4st[par][:, oc * 128:(oc + 1) * 128],
                                    be1cols[:, icx:icx + 1],
                                    start=(icx == 0 and oc == 0),
                                    stop=(icx == DC - 1 and oc == FFC - 1),
                                    skip_group_check=True)
                return last
            e_b1mm = E.op("tensor", b1mms, after=[e_lw1, e_b1mm], inc=CT)
            e_w1c[par] = E.op("vector", lambda v, icx=icx, par=par: v.tensor_scalar_mul(
                w1b[:, icx * FF:(icx + 1) * FF], w4st[par][:], g1cols[:, icx:icx + 1]),
                after=[e_lw1, e_b1mm], inc=CV)

        def b1fin(v):
            v.drain()
            v.tensor_add(b1cols[:], b1cols[:], pb1[:, 0:FFC])
            v.drain()
            return v.tensor_copy(b1cols_bf[:], b1cols[:])
        e_b1e = E.op("vector", b1fin, after=[e_b1mm, e_b1raw], inc=CV)
        e_b1t2 = E.op("tensor", lambda t: t.transpose(pb1bf[:, 0:128], b1cols_bf[:],
                                                      c_id_bf[:]), after=[e_b1e], inc=CT)
        e_b1e2 = E.op("vector", lambda v: v.tensor_copy(b1tr[:], pb1bf[:, 0:128]),
                      after=[e_b1t2], inc=CV)
        e_b1sp = E.dma("scalar", hb1p.rearrange("(c p) -> c p", p=128), b1tr[:],
                       sMisc, after=[e_b1e2], serial=True)
        e_b1r = E.dma("sync", r_b1p[:], hb1p[None, :], sMisc, after=[e_b1sp], serial=True)

        e_w2c = [None, None, None]
        for icx in range(FFC):
            par = icx % 3
            stg = w4st3 if par == 2 else w4st[par]
            e_lw2 = E.dma(("sync", "scalar", "gpsimd")[par],
                          stg[:, 0:D], dw2[icx * 128:(icx + 1) * 128, :], sW4[par],
                          after=[e_w2c[par], e_b1sp if icx < 3 else None])
            e_w2c[par] = E.op("vector", lambda v, icx=icx, stg=stg: v.tensor_copy(
                w2b[:, icx * D:(icx + 1) * D], stg[:, 0:D]), after=[e_lw2], inc=CV)

        p4pro.close()

        lnfm = [sb(f"lnfm{i}", [128, DC * TS4], BF16, stack=p4) for i in range(2)]
        lntm = [sb("lntm0", [128, CPT4 * D], BF16, stack=p4)] * 2
        h1b = sb("h1b", [128, FFC * TS4], BF16, stack=p4)
        r2b = sb("r2b", [128, D], F32, stack=p4)
        sbuf_s = [sb("sbs0", [128, CPT4 * D], F32, stack=p4)] * 2
        stt4 = sb("stt4", [128, 12], F32, stack=p4)
        agg4 = sb("agg4", [128, CPT4 * 2], F32, stack=p4)
        rstd4 = sb("rstd4", [128, CPT4], F32, stack=p4)
        negmr4 = sb("negmr4", [128, CPT4], F32, stack=p4)
        ph1 = psum("ph1", [128, 2 * 1024], F32, stack=p4)
        py = psum("py", [128, 1024], F32, stack=p4)
        ptr4 = psum("ptr4", [128, 1024], BF16, stack=p4)

        e_l4a = [None, None]
        e_l4b = [None, None]
        e_h1ev = [None, None]
        e_ymm = [None, None]
        e_sdone = [None, None]
        e_ssp = [None, None]
        e_r2 = None
        e_sdone_last = None
        e_h1mm_tile = {}
        e_ymm_tile = {}
        e_r2_tile = {}
        e_gel_tile = {}

        for j4 in range(NT4):
            par = j4 % 2
            t0 = j4 * TS4
            pro_done = [e_w2c[0], e_w2c[1], e_w2c[2]] if j4 < 2 else []
            e_l4b[par] = E.dma("sync", lntm[par][:], tm_view(hlnc1, t0, TS4), sL[par],
                               after=[e_r2_tile.get(j4 - 1)] + pro_done)
            e_tr4e = None
            for c in range(CPT4):
                def trp4(t, par=par, c=c):
                    last = None
                    for icx in range(DC):
                        last = t.transpose(ptr4[:, icx * 128:(icx + 1) * 128],
                                           lntm[par][:, c * D + icx * 128: c * D + (icx + 1) * 128],
                                           c_id_bf[:])
                    return last
                e_trm4 = E.op("tensor", trp4, after=[e_l4b[par], e_tr4e,
                                                    e_h1mm_tile.get(j4 - 2) if c == 0 else None], inc=CT)
                def trev4(v, par=par, c=c):
                    return v.tensor_copy(
                        lnfm[par][:].rearrange("p (i t) -> p i t", t=TS4)[:, :, c * 128:(c + 1) * 128],
                        ptr4[:].rearrange("p (i w) -> p i w", w=128))
                e_tr4e = E.op("vector", trev4, after=[e_trm4], inc=CV)
            e_l4a[par] = e_tr4e

            for grp in range(FFC // 4):
                gpar = grp % 2

                def h1mm(t, par=par, grp=grp, gp=gpar):
                    last = None
                    for oci in range(4):
                        oc = grp * 4 + oci
                        dst = ph1[:, gp * 1024 + oci * 256: gp * 1024 + (oci + 1) * 256]
                        for icx in range(DC):
                            t.matmul(dst,
                                     w1b[:, icx * FF + oc * 128: icx * FF + oc * 128 + 128],
                                     lnfm[par][:, icx * TS4:(icx + 1) * TS4],
                                     start=(icx == 0), stop=False)
                        last = t.matmul(dst, r_b1p[0:1, oc * 128:(oc + 1) * 128],
                                        c_ones_bf[0:1, 0:TS4], start=False, stop=True)
                    return last
                e_h1 = E.op("tensor", h1mm, after=[e_l4a[par], e_w1c[0], e_w1c[1], e_w1c[2], e_b1r,
                                                   e_h1ev[gpar]], inc=CT)
                e_h1mm_tile[j4] = e_h1
                e_h1ev[gpar] = E.op("scalar", lambda s, grp=grp, gp=gpar: s.activation(
                    h1b[:, grp * 4 * TS4:(grp + 1) * 4 * TS4],
                    ph1[:, gp * 1024:(gp + 1) * 1024], AF.Gelu),
                    after=[e_h1, e_ymm_tile.get(j4 - 1)], inc=CS)
                e_gel_tile[j4] = e_h1ev[gpar]

            for c in range(CPT4):
                ypar = c % 2

                def ymm(t, c=c):
                    last = None
                    for icx in range(FFC):
                        lhs = h1b[:, icx * TS4 + c * 128: icx * TS4 + (c + 1) * 128]
                        for half in range(2):
                            t.matmul(py[:, half * 512:(half + 1) * 512],
                                     lhs, w2b[:, icx * D + half * 512: icx * D + (half + 1) * 512],
                                     start=(icx == 0), stop=False)
                    for half in range(2):
                        last = t.matmul(py[:, half * 512:(half + 1) * 512],
                                        c_ones_bf[0:1, 0:128],
                                        r_b2pp[0:1, half * 512:(half + 1) * 512],
                                        start=False, stop=True)
                    return last
                e_y = E.op("tensor", ymm, after=[e_gel_tile.get(j4), e_w2c[0], e_w2c[1], e_w2c[2],
                                                 e_ymm[0], e_ymm[1]], inc=CT)
                e_ymm_tile[j4] = e_y

                def r2calc(v, par=par, c=c):
                    v.drain()
                    v.tensor_tensor(r2b[:], lntm[par][:, c * D:(c + 1) * D], g1b[:], ALU.mult)
                    v.drain()
                    return v.tensor_add(r2b[:], r2b[:], py[:, 0:1024])
                e_r2 = E.op("vector", r2calc, after=[e_y, e_l4b[par], e_sdone_last], inc=CV)
                e_r2_tile[j4] = e_r2
                e_ymm[ypar] = e_r2

                def stats4(v, c=c):
                    v.drain()
                    v.bn_stats(stt4[:, 0:6], r2b[:, 0:512])
                    v.bn_stats(stt4[:, 6:12], r2b[:, 512:1024])
                    v.drain()
                    v.bn_aggr(agg4[:, 2 * c:2 * c + 2], stt4[:])
                    v.drain()
                    return v.tensor_scalar_add(rstd4[:, c:c + 1], agg4[:, 2 * c + 1:2 * c + 2],
                                               LN_EPS)
                e_s4 = E.op("vector", stats4, after=[e_r2], inc=CV)
                e_sq4 = E.op("scalar", lambda s, c=c: s.activation(
                    rstd4[:, c:c + 1], rstd4[:, c:c + 1], AF.Sqrt), after=[e_s4], inc=CS)

                def fin4(v, c=c):
                    v.reciprocal(rstd4[:, c:c + 1], rstd4[:, c:c + 1])
                    v.drain()
                    v.tensor_tensor(negmr4[:, c:c + 1], agg4[:, 2 * c:2 * c + 1],
                                    rstd4[:, c:c + 1], ALU.mult)
                    v.drain()
                    return v.tensor_scalar_mul(negmr4[:, c:c + 1], negmr4[:, c:c + 1], -1.0)
                e_f4 = E.op("vector", fin4, after=[e_sq4], inc=CV)
                e_lnc2 = E.op("scalar", lambda s, c=c: s.activation(
                    r2b[:], r2b[:], AF.Identity, bias=negmr4[:, c:c + 1], scale=rstd4[:, c:c + 1]),
                    after=[e_f4, e_sdone_last], inc=CS)

                def scalc(v, par=par, c=c):
                    v.drain()
                    slot = sbuf_s[par][:, c * D:(c + 1) * D]
                    v.tensor_tensor(slot, r2b[:], g2b[:], ALU.mult)
                    v.drain()
                    return v.tensor_add(slot, slot, be2b[:])
                e_sdone[par] = E.op("vector", scalc,
                                    after=[e_lnc2, e_ssp[0] if c == 0 else None,
                                           e_ssp[1] if c == 0 else None], inc=CV)
                e_sdone_last = e_sdone[par]
            e_ssp[par] = E.dma("scalar", tm_view(hs, t0, TS4), sbuf_s[par][:],
                               sSp[par], after=[e_sdone[par]])

        E.barrier(GATE, {
            "vector": [e_sdone[0], e_sdone[1]],
            "scalar": [E.stream_ev(sSp[0]), E.stream_ev(sSp[1])],
            "tensor": [e_ymm_tile[NT4 - 1]],
            "gpsimd": [],
            "sync": [],
        })
        p4.close()

        p5 = contextlib.ExitStack()
        wdqT = sb("wdqT", [128, ICC * D], BF16, stack=p5)
        pfb = [sb(f"pfb{i}", [128, TS], F32, stack=p5) for i in range(6)]
        rt5b = sb("rt5b", [128, TS], F32, stack=p5)
        rhb = [sb(f"rhb{i}", [128, ICC * TS], BF16, stack=p5) for i in range(2)]
        sld = [sb(f"sld{i}", [128, D], F32, stack=p5) for i in range(2)]
        xld = [sb(f"xld{i}", [128, D], F32, stack=p5) for i in range(2)]
        gbuf = sb("gbuf", [128, D], F32, stack=p5)
        tbuf = sb("tbuf", [128, D], F32, stack=p5)
        obuf = [sb(f"obuf{i}", [128, D], F32, stack=p5) for i in range(2)]
        rt5 = sb("rt5", [128, TS], F32, stack=p5)
        pd = psum("pd", [128, 2 * 1024], F32, stack=p5)
        pbx5 = psum("pbx5", [128, 512], F32, stack=p5)

        def sdcalc(v):
            v.drain()
            v.tensor_tensor(sc_tmp[0:1, 6:7], sc_tmp[0:1, 5:6], sc_clip[0:1, 1:2], ALU.mult)
            v.drain()
            v.tensor_scalar_mul(sc_tmp[0:1, 6:7], sc_tmp[0:1, 6:7], 1.0 / 127.0)
            v.drain()
            return v.reciprocal(sc_tmp[0:1, 7:8], sc_tmp[0:1, 6:7])
        e_sdg = E.op("vector", sdcalc, after=[e_cback2], inc=CV)

        e_bx5 = E.op("tensor", lambda t: t.matmul(pbx5[:, 0:2], c_ones_f[0:1, 0:128],
                                                  sc_tmp[0:1, 6:8], start=True, stop=True),
                     after=[e_sdg], inc=CT)

        def sdevac(v):
            v.tensor_copy(sc_sd[:], pbx5[:, 0:1])
            return v.tensor_copy(sc_rsd[:], pbx5[:, 1:2])
        e_sdb = E.op("vector", sdevac, after=[e_bx5], inc=CV)

        e_wdT = None
        wd_sp_done0 = E.stream_ev(sWs[0])
        wd_sp_done1 = E.stream_ev(sWs[1])
        for ici in range(ICC):
            e_wdT = E.dma("sync", wdqT[:, ici * D:(ici + 1) * D],
                          hwdq[:, ici * 128:(ici + 1) * 128], sWt,
                          after=[wd_sp_done0, wd_sp_done1], transpose=True)

        e_pl = [None, None]
        e_rh = {}
        e_a1 = {}
        e_a2 = {}
        e_clip = {}
        e_dmm = [None, None]
        e_ost = [None, None]
        e_out_last = None
        e_outv_tile = {}
        e_dmm_tile = {}
        e_sxl = [None, None]

        def emit_rh(j):
            t0 = j * TS
            for ici in range(ICC):
                gci = j * ICC + ici
                ppar = gci % 6
                p2par = gci % 2
                prev_a1 = e_a1.get(gci - 6)
                e_pl6 = E.dma("sync", pfb[ppar][:],
                              hp[ici * 128:(ici + 1) * 128, t0:t0 + TS], sPf[ppar],
                              after=[prev_a1])
                e_a1[gci] = E.op("scalar", lambda s, pp=ppar, p2=p2par: s.activation(
                    (rt5 if p2 == 0 else rt5b)[:], pfb[pp][:], AF.Identity,
                    scale=sc_rsd[:, 0:1], bias=c_magic[:, 0:1]),
                    after=[e_pl6, e_sdb, e_clip.get(gci - 2)], inc=CS)
                e_a2[gci] = E.op("vector", lambda v, p2=p2par: v.tensor_scalar_add(
                    (rt5 if p2 == 0 else rt5b)[:], (rt5 if p2 == 0 else rt5b)[:], -MAGIC),
                    after=[e_a1[gci], e_clip.get(gci - 2)], inc=CV)
                e_clip[gci] = E.op("vector", lambda v, j=j, ici=ici, p2=p2par: v.tensor_scalar(
                    rhb[j % 2][:, ici * TS:(ici + 1) * TS], (rt5 if p2 == 0 else rt5b)[:],
                    127.0, -128.0, ALU.min, ALU.max),
                    after=[e_a2[gci], e_clip.get(gci - 1),
                           e_dmm_tile.get(j - 2) if ici == 0 else None], inc=CV)
                e_rh[(j, ici)] = e_clip[gci]

        def emit_down(j):
            t0 = j * TS
            par = j % 2
            for c in range(CPT):
                dpar = c % 2

                def dmm(t, par=par, c=c, dp=dpar):
                    last = None
                    for ici in range(ICC):
                        lhs = rhb[par][:, ici * TS + c * 128: ici * TS + (c + 1) * 128]
                        for half in range(2):
                            last = t.matmul(
                                pd[:, dp * 1024 + half * 512: dp * 1024 + (half + 1) * 512],
                                lhs, wdqT[:, ici * D + half * 512: ici * D + (half + 1) * 512],
                                start=(ici == 0), stop=(ici == ICC - 1))
                    return last
                e_d = E.op("tensor", dmm, after=[e_rh[(j, ICC - 1)], e_wdT, e_dmm[dpar]], inc=CT)
                e_dmm_tile[j] = e_d
                e_gd = E.op("scalar", lambda s, dp=dpar: s.activation(
                    gbuf[:], pd[:, dp * 1024:(dp + 1) * 1024], AF.Silu, scale=sc_sd[:, 0:1]),
                    after=[e_d, e_sdb, e_out_last2[0]], inc=CS)
                gc5 = j * CPT + c
                cp5 = gc5 % 2
                e_sl = E.dma("sync", sld[cp5][:], tm_view(hs, t0 + c * 128, 128), sL[cp5],
                             after=[e_sxl[cp5]])
                e_xl = E.dma("sync", xld[cp5][:], tm_view(dx, t0 + c * 128, 128), sLc[cp5],
                             after=[e_sxl[cp5]])

                def outcalc(v, cp=cp5):
                    v.drain()
                    v.tensor_tensor(tbuf[:], sld[cp][:], gbuf[:], ALU.mult)
                    v.drain()
                    return v.tensor_add(obuf[cp][:], tbuf[:], xld[cp][:])
                e_out = E.op("vector", outcalc, after=[e_gd, e_sl, e_xl, e_ost[cp5]], inc=CV)
                e_out_last2[0] = e_out
                e_outv_tile[j] = e_out
                e_dmm[dpar] = e_out
                e_sxl[cp5] = e_out
                e_ost[cp5] = E.dma("scalar", tm_view(dout, t0 + c * 128, 128), obuf[cp5][:],
                                   sOut[cp5], after=[e_out])

        e_out_last2 = [None]
        for j in range(NT + 1):
            if j < NT:
                emit_rh(j)
            if j >= 1:
                emit_down(j - 1)

        E.steps["gpsimd"].append(lambda e, v0=None: e.wait_ge(sOut[0], E.vals[sOut[0].name]))
        E.steps["gpsimd"].append(lambda e: e.wait_ge(sOut[1], E.vals[sOut[1].name]))

        p5.close()
        E.replay(blk)

    return nc


_NC_CACHE = {}


def kernel(**inputs):
    S = inputs["L0_attn_src"].shape[1]
    if S not in _NC_CACHE:
        _NC_CACHE[S] = build(S=S)
    nc = _NC_CACHE[S]
    x = np.ascontiguousarray(inputs["L0_attn_src"], dtype=np.float32)
    clip2 = np.stack([np.asarray(inputs["L0_ffn_clip_x"]).reshape(-1)[0],
                      np.asarray(inputs["L0_ffn_clip_down"]).reshape(-1)[0]]).reshape(1, 2).astype(np.float32)
    base = {
        "wq": inputs["L0_attn_Wq"], "wk": inputs["L0_attn_Wk"], "wv": inputs["L0_attn_Wv"],
        "wo": inputs["L0_attn_Wo"], "bq": inputs["L0_attn_bq"], "bk": inputs["L0_attn_bk"],
        "bv": inputs["L0_attn_bv"], "bo": inputs["L0_attn_bo"], "w1": inputs["L0_attn_W1"],
        "b1": inputs["L0_attn_b1"], "w2": inputs["L0_attn_W2"], "b2": inputs["L0_attn_b2"],
        "g1": inputs["L0_attn_g1"], "be1": inputs["L0_attn_be1"], "g2": inputs["L0_attn_g2"],
        "be2": inputs["L0_attn_be2"], "clip2": clip2,
        "wg": inputs["L0_ffn_w_gate"], "wu": inputs["L0_ffn_w_up"], "wd": inputs["L0_ffn_w_down"],
    }
    base = {k: np.ascontiguousarray(v, dtype=np.float32) for k, v in base.items()}
    in_maps = [dict(base, x=np.ascontiguousarray(x[i])) for i in range(NCORES)]
    res = run_bass_kernel_spmd(nc, in_maps, core_ids=list(range(NCORES)))
    return np.stack([res.results[i]["out"] for i in range(NCORES)], axis=0)


# version 36
# speedup vs baseline: 1.0296x; 1.0296x over previous
import contextlib
import numpy as np
import ml_dtypes

import concourse.bass as bass
import concourse.mybir as mybir
from concourse.bass_utils import run_bass_kernel_spmd

F32 = mybir.dt.float32
BF16 = mybir.dt.bfloat16
AF = mybir.ActivationFunctionType
ALU = mybir.AluOpType
AX = mybir.AxisListType

MAGIC = float(3 * 2**22)
LN_EPS = 1e-5
Z_EPS = 1e-6
D = 1024
H = 16
FF = 4096
I_FF = 2816
NCORES = 8

ENGINES = ("sync", "tensor", "vector", "scalar", "gpsimd")


class Em:

    def __init__(self, nc):
        self.nc = nc
        self.steps = {e: [] for e in ENGINES}
        self.semctx = contextlib.ExitStack()
        self.vals = {}
        self.waited = {e: {} for e in ENGINES}

    def sem(self, name):
        h = self.semctx.enter_context(self.nc.semaphore(name))
        self.vals[name] = 0
        return h

    def _wait(self, eng, ev):
        semh, val = ev
        if self.waited[eng].get(semh.name, 0) >= val:
            return
        self.waited[eng][semh.name] = val
        self.steps[eng].append(lambda e, s=semh, v=val: e.wait_ge(s, v))

    def op(self, eng, fn, after=(), inc=None):
        for ev in after:
            if ev is not None:
                self._wait(eng, ev)
        ev_out = None
        dr = eng in ("scalar", "gpsimd")
        if inc is not None:
            self.vals[inc.name] += 1
            ev_out = (inc, self.vals[inc.name])

            def step(e, fn=fn, inc=inc, dr=dr):
                if dr:
                    e.drain()
                last = fn(e)
                assert last is not None, "op with inc= must return the last instruction"
                last.then_inc(inc, 1)
            self.steps[eng].append(step)
        else:
            def step0(e, fn=fn, dr=dr):
                if dr:
                    e.drain()
                fn(e)
            self.steps[eng].append(step0)
        return ev_out

    def dma(self, eng, out, in_, stream, after=(), transpose=False, serial=False):
        if serial:
            self._wait(eng, (stream, self.vals[stream.name]))
        for ev in after:
            if ev is not None:
                self._wait(eng, ev)
        self.vals[stream.name] += 16
        val = self.vals[stream.name]

        def step(e, out=out, in_=in_, stream=stream, transpose=transpose):
            if transpose:
                e.dma_start_transpose(out=out, in_=in_).then_inc(stream, 16)
            else:
                e.dma_start(out=out, in_=in_).then_inc(stream, 16)
        self.steps[eng].append(step)
        return (stream, val)

    def drain(self, eng):
        self.steps[eng].append(lambda e: e.drain())

    def stream_ev(self, stream):
        return (stream, self.vals[stream.name])

    def barrier(self, gate, per_engine_events):
        tgt = self.vals[gate.name] + 5
        for eng in ENGINES:
            for ev in per_engine_events.get(eng, ()):
                if ev is not None:
                    self._wait(eng, ev)
            self.vals[gate.name] += 1
            self.steps[eng].append(lambda e, g=gate: e.sem_inc(g, 1))
            self._wait(eng, (gate, tgt))

    def replay(self, blk):
        for eng in ENGINES:
            body = getattr(blk, eng)
            steps = self.steps[eng]

            def run(e, steps=steps):
                for s in steps:
                    s(e)
            body(run)


def build(S=4096, n_cores=NCORES, dbg=False):
    TS = 512
    NT = S // TS
    CPT = TS // 128
    NCH = S // 128
    DC = D // 128
    FFC = FF // 128
    ICC = I_FF // 128
    TS4 = 256
    NT4 = S // TS4
    CPT4 = TS4 // 128

    nc = bass.Bass("TRN2", target_bir_lowering=False)

    dx = nc.dram_tensor("x", [S, D], F32, kind="ExternalInput").ap()
    dwq = nc.dram_tensor("wq", [D, D], F32, kind="ExternalInput").ap()
    dwk = nc.dram_tensor("wk", [D, D], F32, kind="ExternalInput").ap()
    dwv = nc.dram_tensor("wv", [D, D], F32, kind="ExternalInput").ap()
    dwo = nc.dram_tensor("wo", [D, D], F32, kind="ExternalInput").ap()
    dbq = nc.dram_tensor("bq", [D], F32, kind="ExternalInput").ap()
    dbk = nc.dram_tensor("bk", [D], F32, kind="ExternalInput").ap()
    dbv = nc.dram_tensor("bv", [D], F32, kind="ExternalInput").ap()
    dbo = nc.dram_tensor("bo", [D], F32, kind="ExternalInput").ap()
    dw1 = nc.dram_tensor("w1", [D, FF], F32, kind="ExternalInput").ap()
    db1 = nc.dram_tensor("b1", [FF], F32, kind="ExternalInput").ap()
    dw2 = nc.dram_tensor("w2", [FF, D], F32, kind="ExternalInput").ap()
    db2 = nc.dram_tensor("b2", [D], F32, kind="ExternalInput").ap()
    dg1 = nc.dram_tensor("g1", [D], F32, kind="ExternalInput").ap()
    dbe1 = nc.dram_tensor("be1", [D], F32, kind="ExternalInput").ap()
    dg2 = nc.dram_tensor("g2", [D], F32, kind="ExternalInput").ap()
    dbe2 = nc.dram_tensor("be2", [D], F32, kind="ExternalInput").ap()
    dclip = nc.dram_tensor("clip2", [1, 2], F32, kind="ExternalInput").ap()
    dwg = nc.dram_tensor("wg", [I_FF, D], F32, kind="ExternalInput").ap()
    dwu = nc.dram_tensor("wu", [I_FF, D], F32, kind="ExternalInput").ap()
    dwd = nc.dram_tensor("wd", [D, I_FF], F32, kind="ExternalInput").ap()
    dout = nc.dram_tensor("out", [S, D], F32, kind="ExternalOutput").ap()

    identf = nc.inline_tensor(np.eye(128, dtype=np.float32), "identf").ap()
    bc2_np = np.zeros((2, 128), dtype=ml_dtypes.bfloat16)
    bc2_np[0, 0:64] = 1.0
    bc2_np[1, 64:128] = 1.0
    dbc2 = nc.inline_tensor(bc2_np, "bc2c").ap()
    DCl = D // 128
    sel_np = np.zeros((16, DCl * 128), dtype=ml_dtypes.bfloat16)
    for occ in range(DCl):
        for pp in range(128):
            sel_np[2 * occ + pp // 64, occ * 128 + pp] = 1.0
    dsel = nc.inline_tensor(sel_np, "zselc").ap()

    ko = dict(kind="ExternalOutput") if dbg else {}
    hxnf = nc.dram_tensor("hxnf", [S, D], F32, **ko).ap()
    hxnb = nc.dram_tensor("hxnb", [D, S], BF16, **ko).ap()
    hlnc1 = nc.dram_tensor("hlnc1", [S, D], BF16, **ko).ap()
    hp = nc.dram_tensor("hp", [I_FF, S], F32, **ko).ap()
    hs = nc.dram_tensor("hs", [S, D], F32, **ko).ap()
    hwgq = nc.dram_tensor("hwgq", [I_FF, D], BF16, **ko).ap()
    hwuq = nc.dram_tensor("hwuq", [I_FF, D], BF16, **ko).ap()
    hwdq = nc.dram_tensor("hwdq", [D, I_FF], BF16, **ko).ap()
    hb1p = nc.dram_tensor("hb1p", [FF], BF16).ap()
    cc_space = "Shared" if n_cores > 4 else "Local"
    ccx_in = nc.dram_tensor("ccx_in", [1, 16], F32).ap()
    ccx_out = nc.dram_tensor("ccx_out", [1, 16], F32, addr_space=cc_space).ap()
    ccd_in = nc.dram_tensor("ccd_in", [1, 16], F32).ap()
    ccd_out = nc.dram_tensor("ccd_out", [1, 16], F32, addr_space=cc_space).ap()
    if dbg:
        hsc = nc.dram_tensor("hsc", [1, 16], F32, kind="ExternalOutput").ap()
        hrfm = nc.dram_tensor("hrfm", [128, 8 * 512], mybir.dt.bfloat16, kind="ExternalOutput").ap()

    ctx = contextlib.ExitStack()
    E = Em(nc)

    def sb(name, shape, dtype, stack=None):
        return (stack or ctx).enter_context(nc.sbuf_tensor(name, list(shape), dtype))

    def psum(name, shape, dtype=F32, stack=None):
        return (stack or ctx).enter_context(nc.psum_tensor(name, list(shape), dtype))

    def tm_view(hbm, t0, n):
        return hbm[t0:t0 + n, :].rearrange("(c p) d -> p c d", p=128)

    with ctx:
        c_id = sb("c_id", [128, 128], F32)
        c_id_bf = sb("c_id_bf", [128, 128], BF16)
        c_ones_bf = sb("c_ones_bf", [1, 512], BF16)
        c_ones_f = sb("c_ones_f", [1, 512], F32)
        c_bc2 = sb("c_bc2", [2, 128], BF16)
        r_bk = sb("r_bk", [1, D], BF16)
        r_bv = sb("r_bv", [1, D], BF16)
        r_bo = sb("r_bo", [1, D], BF16)
        r_b2pp = sb("r_b2pp", [1, D], BF16)
        bq_cols = sb("bq_cols", [128, DC], F32)
        g1cols = sb("g1cols", [128, DC], F32)
        be1cols = sb("be1cols", [128, DC], F32)
        g1b = sb("g1b", [128, D], F32)
        g2b = sb("g2b", [128, D], F32)
        be2b = sb("be2b", [128, D], F32)
        sc_clip = sb("sc_clip", [1, 2], F32)
        sc_sx = sb("sc_sx", [128, 1], F32)
        sc_rsx = sb("sc_rsx", [128, 1], F32)
        sc_sd = sb("sc_sd", [128, 1], F32)
        sc_rsd = sb("sc_rsd", [128, 1], F32)
        amax_sl = sb("amax_sl", [128, NCH], F32)
        amax_red = sb("amax_red", [128, 1], F32)
        pmax_sl = sb("pmax_sl", [128, NT * ICC], F32)
        pmax_red = sb("pmax_red", [128, 1], F32)
        sc_tmp = sb("sc_tmp", [1, 16], F32)
        c_magic = sb("c_magic", [128, 1], F32)
        c_nmagic = sb("c_nmagic", [128, 1], F32)
        s_qkvo = contextlib.ExitStack()
        c_sel = sb("c_sel", [16, DC * 128], BF16, stack=s_qkvo)
        kvbd = sb("kvbd", [128, DC * 128], BF16, stack=s_qkvo)
        zlhs = sb("zlhs", [128, DC * 16], BF16, stack=s_qkvo)
        wq_bf = sb("wq_bf", [128, DC * D], BF16, stack=s_qkvo)
        wk_bf = sb("wk_bf", [128, DC * D], BF16, stack=s_qkvo)
        wv_bf = sb("wv_bf", [128, DC * D], BF16, stack=s_qkvo)
        wo_bf = sb("wo_bf", [128, DC * D], BF16, stack=s_qkvo)

        GATE = E.sem("gate")
        CT = E.sem("ct")
        CV = E.sem("cv")
        CS = E.sem("cs")
        CC = E.sem("cc")
        CG = E.sem("cg")
        sL = [E.sem(f"sl{i}") for i in range(2)]
        sLb = [E.sem(f"slb{i}") for i in range(2)]
        sLc = [E.sem(f"slc{i}") for i in range(2)]
        sLb2 = [E.sem(f"slb2{i}") for i in range(2)]
        sSp = [E.sem(f"ssp{i}") for i in range(3)]
        sSp2 = [E.sem(f"ssp2{i}") for i in range(3)]
        sW = [E.sem(f"sw{i}") for i in range(3)]
        sWs = [E.sem(f"sws{i}") for i in range(3)]
        sWt = E.sem("swt")
        sWt2 = E.sem("swt2")
        sMisc = E.sem("smisc")
        sCc = E.sem("scc")
        sOut = [E.sem(f"souts{i}") for i in range(2)]
        sPf = [E.sem(f"spf{i}") for i in range(6)]
        sW4 = [E.sem(f"sw4{i}") for i in range(3)]

        blk = ctx.enter_context(nc.Block())

        p0 = contextlib.ExitStack()
        g1row = sb("g1row", [1, D], F32, stack=p0)
        g2row = sb("g2row", [1, D], F32, stack=p0)
        be2row = sb("be2row", [1, D], F32, stack=p0)
        be1row = sb("be1row", [1, D], F32, stack=p0)
        rows = sb("rows", [1, 4 * D], F32, stack=p0)
        OFF_BK, OFF_BV, OFF_BO, OFF_B2 = 0, D, 2 * D, 3 * D
        for dst, srcap in [(c_id[:], identf[:, :]), (c_bc2[:], dbc2[:, :]),
                           (sc_clip[:], dclip[:, :]),
                           (c_sel[:], dsel[:, :]),
                           (g1row[:], dg1[None, :]), (g2row[:], dg2[None, :]),
                           (be2row[:], dbe2[None, :]), (be1row[:], dbe1[None, :])]:
            E.dma("sync", dst, srcap, sMisc, serial=True)
        colstg = sb("colstg", [DC, 128], F32, stack=p0)
        pcols0 = psum("pcols0", [128, 512], F32, stack=p0)
        e_cid = E.stream_ev(sMisc)
        e_colprev = None
        for dsrc, dstc in [(dg1, g1cols), (dbe1, be1cols), (dbq, bq_cols)]:
            e_cl = E.dma("sync", colstg[:], dsrc.rearrange("(c p) -> c p", p=128),
                         sMisc, after=[e_colprev], serial=True)
            e_ct = E.op("tensor", lambda t: t.transpose(pcols0[:, 0:DC], colstg[:], c_id[0:DC, 0:DC]),
                        after=[e_cl, e_cid, e_colprev], inc=CT)
            e_colprev = E.op("vector", lambda v, d=dstc: v.tensor_copy(d[:], pcols0[:, 0:DC]),
                             after=[e_ct], inc=CV)
        for srcd, off, n in [(dbk, OFF_BK, D), (dbv, OFF_BV, D), (dbo, OFF_BO, D),
                             (db2, OFF_B2, D)]:
            E.dma("sync", rows[0:1, off:off + n], srcd[None, :], sMisc, serial=True)
        ldall = E.stream_ev(sMisc)

        e_idbf = E.op("vector", lambda v: v.tensor_copy(c_id_bf[:], c_id[:]),
                      after=[ldall], inc=CV)
        e_ms = E.op("vector", lambda v: (
            v.memset(amax_sl[:], 0.0),
            v.memset(c_magic[:], MAGIC),
            v.memset(c_nmagic[:], -MAGIC),
            v.memset(c_ones_bf[:], 1.0),
            v.memset(c_ones_f[:], 1.0),
            v.memset(zlhs[:], 0.0),
            v.memset(kvbd[:], 0.0),
            v.memset(sc_tmp[:], 0.0))[-1], inc=CV)
        e_rows = E.op("vector", lambda v: (
            v.tensor_copy(r_bk[:], rows[0:1, OFF_BK:OFF_BK + D]),
            v.tensor_copy(r_bv[:], rows[0:1, OFF_BV:OFF_BV + D]),
            v.tensor_copy(r_bo[:], rows[0:1, OFF_BO:OFF_BO + D]),
            v.tensor_add(rows[0:1, OFF_B2:OFF_B2 + D], rows[0:1, OFF_B2:OFF_B2 + D],
                         be1row[0:1, :]),
            v.drain(),
            v.tensor_copy(r_b2pp[:], rows[0:1, OFF_B2:OFF_B2 + D]))[-1],
            after=[ldall], inc=CV)
        e_sig = E.op("scalar", lambda s: s.activation(sc_clip[:], sc_clip[:], AF.Sigmoid),
                     after=[ldall], inc=CS)

        pbc = psum("pbc", [128, 512], F32, stack=p0)
        prev_evac = None
        for dst, rowap in [(g1b, g1row), (g2b, g2row), (be2b, be2row)]:
            for half in range(2):
                e_mm = E.op("tensor", lambda t, r=rowap, h=half: t.matmul(
                    pbc[:, :], c_ones_f[0:1, 0:128], r[0:1, h * 512:(h + 1) * 512],
                    start=True, stop=True), after=[ldall, prev_evac, e_ms], inc=CT)
                prev_evac = E.op("vector", lambda v, d=dst, h=half: v.tensor_copy(
                    d[:, h * 512:(h + 1) * 512], pbc[:, :]), after=[e_mm], inc=CV)

        wst = [sb(f"wst{i}", [128, D], F32, stack=p0) for i in range(2)]
        prev_cast = [None, None]
        for wi, (dsrc, dst) in enumerate([(dwk, wk_bf), (dwv, wv_bf)]):
            for icx in range(DC):
                idx = wi * DC + icx
                par = idx % 2
                e_l = E.dma("sync" if par == 0 else "scalar",
                            wst[par][:], dsrc[icx * 128:(icx + 1) * 128, :],
                            sLb2[par], after=[prev_cast[par]])
                prev_cast[par] = E.op("vector", lambda v, d=dst, i=icx, p=par: v.tensor_copy(
                    d[:, i * D:(i + 1) * D], wst[p][:]), after=[e_l], inc=CV)

        E.barrier(GATE, {"vector": [prev_cast[0], prev_cast[1], e_ms, e_rows, prev_evac],
                         "scalar": [e_sig]})
        p0.close()

        TS1 = 256
        NT1 = S // TS1
        CPT1 = TS1 // 128
        p1 = contextlib.ExitStack()
        xbuf = [sb(f"xbuf{i}", [128, CPT1 * D], F32, stack=p1) for i in range(2)]
        xnf_t = [sb(f"xnf{i}", [128, CPT1 * D], F32, stack=p1) for i in range(3)]
        xnfm = [sb(f"xnfm{i}", [128, DC * TS1], BF16, stack=p1) for i in range(2)]
        stt = sb("stt", [128, 12], F32, stack=p1)
        agg = [sb(f"agg{i}", [128, CPT1 * 2], F32, stack=p1) for i in range(2)]
        rstd_t = [sb(f"rstd_t{i}", [128, CPT1], F32, stack=p1) for i in range(2)]
        negmr_t = [sb(f"negmr_t{i}", [128, CPT1], F32, stack=p1) for i in range(2)]
        kb = [sb(f"kb{i}", [128, D], BF16, stack=p1) for i in range(2)]
        vaug = [sb(f"vaug{i}", [128, H * 65], BF16, stack=p1) for i in range(2)]
        ka_bf = sb("ka_bf", [128, D], BF16, stack=p1)
        ke_bf = sb("ke_bf", [128, D], BF16, stack=p1)
        wst2 = [sb(f"wst2_{i}", [128, D], F32, stack=p1) for i in range(2)]
        e_qkvo = [None, None]

        def wqo_steps():
            prev_c = [None, None]
            for wi, (dsrc, dst) in enumerate([(dwq, wq_bf), (dwo, wo_bf)]):
                for icx in range(DC):
                    idx = wi * DC + icx
                    par = idx % 2
                    e_l = E.dma("sync" if par == 0 else "scalar",
                                wst2[par][:], dsrc[icx * 128:(icx + 1) * 128, :],
                                sLc[par], after=[prev_c[par]])
                    prev_c[par] = E.op("scalar", lambda s, d=dst, i=icx, p=par: s.activation(
                        d[:, i * D:(i + 1) * D], wst2[p][:], AF.Identity),
                        after=[e_l], inc=CS)
                    e_qkvo[par] = prev_c[par]
                    yield
            while True:
                yield
        wqogen = wqo_steps()

        wqst = [sb(f"wqst{i}", [128, D], F32, stack=p1) for i in range(3)]
        wqout = [sb(f"wqout{i}", [128, D], BF16, stack=p1) for i in range(3)]
        wqsc = sb("wqsc", [128, 6], F32, stack=p1)
        pk = psum("pk", [128, 1024], F32, stack=p1)
        pv = psum("pv", [128, 1024], F32, stack=p1)
        kvA = psum("kvA", [128, 512], F32, stack=p1)
        kvB = psum("kvB", [128, 512], F32, stack=p1)
        ptr1 = psum("ptr1", [128, 1024], F32, stack=p1)

        e_vone = E.op("vector", lambda v: (
            v.memset(vaug[0][:], 1.0), v.memset(vaug[1][:], 1.0))[-1], inc=CV)

        def wq_steps():
            chunks = [(dsrc, hdst, ci) for dsrc, hdst in [(dwg, hwgq), (dwu, hwuq)]
                      for ci in range(ICC)]
            NQ = len(chunks)
            prev_q = [None, None, None]
            prev_mul = [None, None, None]
            e_ld = {}
            for k in range(NQ + 1):
                if k < NQ:
                    dsrc, hdst, ci = chunks[k]
                    par = k % 3
                    e_ld[k] = E.dma("gpsimd", wqst[par][:], dsrc[ci * 128:(ci + 1) * 128, :],
                                    sW[par], after=[prev_mul[par]])
                if k >= 1:
                    dsrc, hdst, ci = chunks[k - 1]
                    par = (k - 1) % 3

                    def qred(v, par=par):
                        v.drain()
                        v.tensor_reduce(wqsc[:, 2 * par:2 * par + 1], wqst[par][:], AX.X, ALU.max,
                                        apply_absolute_value=True)
                        v.drain()
                        v.tensor_scalar_mul(wqsc[:, 2 * par:2 * par + 1],
                                            wqsc[:, 2 * par:2 * par + 1], 1.0 / 127.0)
                        v.drain()
                        return v.reciprocal(wqsc[:, 2 * par + 1:2 * par + 2],
                                            wqsc[:, 2 * par:2 * par + 1])
                    e_r1 = E.op("vector", qred, after=[e_ld[k - 1], prev_mul[par]], inc=CV)
                    yield
                    def qfin(v, par=par):
                        v.drain()
                        v.tensor_scalar(wqst[par][:], wqst[par][:], wqsc[:, 2 * par + 1:2 * par + 2],
                                        MAGIC, ALU.mult, ALU.add)
                        v.drain()
                        return v.tensor_scalar(wqout[par][:], wqst[par][:], MAGIC,
                                               wqsc[:, 2 * par:2 * par + 1],
                                               ALU.subtract, ALU.mult)
                    e_mul = E.op("vector", qfin, after=[e_r1, prev_q[par]], inc=CV)
                    prev_mul[par] = e_mul
                    prev_q[par] = E.dma("gpsimd", hdst[ci * 128:(ci + 1) * 128, :],
                                        wqout[par][:], sWs[par], after=[e_mul])
                yield
            while True:
                yield
        wqgen = wq_steps()

        e_xload = [None, None]
        e_sq_pp = [None, None]
        e_fs_pp = [None, None]
        e_am_pp = [None, None]
        e_xnf = [None, None]
        e_xnb = [None, None]
        e_spf = [None, None]
        e_spb = [None, None]
        e_fmload = [None, None]
        e_tr1e_half = [None, None]
        kv_pend = [None]
        e_tr1ev_t = {}
        e_spf3 = [None, None, None]
        e_apply_c = {}
        e_am3 = [None, None, None]
        e_trp3 = [None, None, None]
        e_apply = {}
        e_sttH = {}
        e_vcH = {}
        e_kvmmh = {}
        e_kfin = {}
        e_vcopy = {}
        e_kvmm = {}
        e_proj_tile = {}

        def emit_stageA(j):
            par = j % 2
            t0 = j * TS1
            e_xload[par] = E.dma("sync", xbuf[par][:], tm_view(dx, t0, TS1), sL[par],
                                 after=[e_xnf[par]])

            def stats(v, par=par):
                v.drain()
                for c in range(CPT1):
                    v.bn_stats(stt[:, 0:6], xbuf[par][:, c * D:c * D + 512])
                    v.bn_stats(stt[:, 6:12], xbuf[par][:, c * D + 512:c * D + 1024])
                    v.drain()
                    v.bn_aggr(agg[par][:, 2 * c:2 * c + 2], stt[:])
                    v.drain()
                v.tensor_scalar_add(
                    rstd_t[par][:], agg[par][:].rearrange("p (c two) -> p c two", two=2)[:, :, 1], LN_EPS)
                v.drain()
                return v.reciprocal(rstd_t[par][:], rstd_t[par][:])
            e_st = E.op("vector", stats, after=[e_xload[par], e_fs_pp[par]], inc=CV)
            e_sq_pp[par] = e_st
            next(wqogen)
            next(wqgen)
            next(wqgen)

        def emit_stageB1(j):
            par = j % 2
            p3 = j % 3
            t0 = j * TS1

            def pre_ln(s, par=par):
                s.activation(rstd_t[par][:], rstd_t[par][:], AF.Sqrt)
                s.drain()
                m_v = agg[par][:].rearrange("p (c two) -> p c two", two=2)
                for c in range(CPT1):
                    s.activation(negmr_t[par][:, c:c + 1], m_v[:, c, 0:1], AF.Identity,
                                 scale=rstd_t[par][:, c:c + 1])
                s.drain()
                return s.activation(negmr_t[par][:], negmr_t[par][:], AF.Identity,
                                    scale=-1.0)
            e_pre = E.op("scalar", pre_ln, after=[e_sq_pp[par], e_fs_pp[par]], inc=CS)

            e_ap = None
            for c in range(CPT1):
                def ap1(s, par=par, p3=p3, c=c):
                    return s.activation(xnf_t[p3][:, c * D:(c + 1) * D],
                                        xbuf[par][:, c * D:(c + 1) * D], AF.Identity,
                                        bias=negmr_t[par][:, c:c + 1],
                                        scale=rstd_t[par][:, c:c + 1])
                e_ap = E.op("scalar", ap1,
                            after=[e_pre, e_spf3[p3] if c == 0 else None,
                                   e_am3[p3] if c == 0 else None,
                                   e_trp3[p3] if c == 0 else None], inc=CS)
                e_apply_c[(j, c)] = e_ap
            e_xnf[par] = e_ap
            e_apply[j] = e_ap
            e_fs_pp[par] = e_ap

            e_spf3[p3] = E.dma("scalar", tm_view(hxnf, t0, TS1), xnf_t[p3][:],
                               sSp[p3], after=[e_ap])
            next(wqgen)
            next(wqgen)

        def emit_kvmm_h(gc, cpar, half):
            def kvmm(t, cp=cpar, half=half, first=(gc == 0), last_=(gc == NCH - 1)):
                out = None
                ptile = kvA if half == 0 else kvB
                for hh in range(8):
                    h = 8 * half + hh
                    g = hh // 2
                    rowb = 64 * (hh % 2)
                    out = t.matmul(ptile[rowb:rowb + 64, g * 65:(g + 1) * 65],
                                   kb[cp][:, h * 64:(h + 1) * 64],
                                   vaug[cp][:].rearrange("p (h w) -> p h w", w=65)[:, h, :],
                                   start=(first and hh in (0, 1)),
                                   stop=(last_ and hh in (6, 7)),
                                   tile_position=(0, rowb),
                                   skip_group_check=True)
                return out
            e_kvmmh[(gc, half)] = E.op("tensor", kvmm,
                                       after=[e_sttH[(gc, half)], e_vcH[(gc, half)]], inc=CT)

        def emit_stageB2a(j):
            par = j % 2
            p3 = j % 3
            t0 = j * TS1
            for c in range(CPT1):
                e_tr1e = None
                for rnd in range(2):
                    half = rnd
                    def trp1(t, p3=p3, c=c, rnd=rnd, half=half):
                        last = None
                        for k in range(4):
                            icx = rnd * 4 + k
                            last = t.transpose(ptr1[:, half * 512 + k * 128: half * 512 + (k + 1) * 128],
                                               xnf_t[p3][:, c * D + icx * 128: c * D + (icx + 1) * 128],
                                               c_id[:])
                        return last
                    e_trm1 = E.op("tensor", trp1, after=[e_apply_c[(j, c)], e_tr1e_half[half], e_spb[par],
                                                         e_proj_tile.get(j - 2) if (c == 0 and rnd == 0) else None], inc=CT)
                    def trev1(s, par=par, c=c, rnd=rnd, half=half):
                        return s.activation(
                            xnfm[par][:].rearrange("p (i t) -> p i t", t=TS1)[:, rnd * 4:(rnd + 1) * 4,
                                                                              c * 128:(c + 1) * 128],
                            ptr1[:, half * 512:(half + 1) * 512].rearrange("p (i w) -> p i w", w=128),
                            AF.Identity)
                    e_tr1e = E.op("scalar", trev1, after=[e_trm1], inc=CS)
                    e_tr1e_half[half] = e_tr1e
                    e_trp3[p3] = e_trm1
                e_tr1ev_t[(j, c)] = e_tr1e
            e_fmload[par] = e_tr1e
            e_spb[par] = E.dma("scalar",
                               hxnb.rearrange("(c p) s -> p c s", p=128)[:, :, t0:t0 + TS1],
                               xnfm[par][:].rearrange("p (i t) -> p i t", t=TS1),
                               sSp2[par], after=[e_tr1e])
            def absmax(v, p3=p3, j=j):
                return v.tensor_reduce(amax_sl[:, j:j + 1], xnf_t[p3][:], AX.X, ALU.max,
                                       apply_absolute_value=True)
            e_am3[p3] = E.op("vector", absmax, after=[e_apply[j]], inc=CV)

        def emit_stageB2b(j):
            par = j % 2
            e_tr1ev = {c: e_tr1ev_t[(j, c)] for c in range(CPT1)}
            for c in range(CPT1):
                gc = j * CPT1 + c
                cpar = gc % 2

                if kv_pend[0] is not None:
                    pgc, pcp = kv_pend[0]
                    emit_kvmm_h(pgc, pcp, 0)
                    emit_kvmm_h(pgc, pcp, 1)
                    kv_pend[0] = None

                for half in range(2):
                    def kproj(t, par=par, c=c, half=half):
                        for icx in range(DC):
                            lhs = xnfm[par][:, icx * TS1 + c * 128: icx * TS1 + (c + 1) * 128]
                            t.matmul(pk[:, half * 512:(half + 1) * 512], lhs,
                                     wk_bf[:, icx * D + half * 512: icx * D + (half + 1) * 512],
                                     start=(icx == 0), stop=False)
                        return t.matmul(pk[:, half * 512:(half + 1) * 512], c_ones_bf[0:1, 0:128],
                                        r_bk[0:1, half * 512:(half + 1) * 512], start=False, stop=True)
                    e_k = E.op("tensor", kproj,
                               after=[e_tr1ev[c], e_sttH.get((gc - 1, half)), e_vone], inc=CT)

                    def elu_s(s, half=half):
                        s.activation(ke_bf[:, half * 512:(half + 1) * 512],
                                     pk[:, half * 512:(half + 1) * 512], AF.Exp)
                        return s.activation(ka_bf[:, half * 512:(half + 1) * 512],
                                            pk[:, half * 512:(half + 1) * 512], AF.Relu)
                    e_el = E.op("scalar", elu_s,
                                after=[e_k, e_sttH.get((gc - 1, half))], inc=CS)
                    e_sttH[(gc, half)] = E.op("vector", lambda v, cp=cpar, half=half: v.scalar_tensor_tensor(
                        kb[cp][:, half * 512:(half + 1) * 512],
                        ke_bf[:, half * 512:(half + 1) * 512], 1.0,
                        ka_bf[:, half * 512:(half + 1) * 512], ALU.min, ALU.add),
                        after=[e_el, e_kvmmh.get((gc - 2, half))], inc=CV)

                for half in range(2):
                    def vproj(t, par=par, c=c, half=half):
                        for icx in range(DC):
                            lhs = xnfm[par][:, icx * TS1 + c * 128: icx * TS1 + (c + 1) * 128]
                            t.matmul(pv[:, half * 512:(half + 1) * 512], lhs,
                                     wv_bf[:, icx * D + half * 512: icx * D + (half + 1) * 512],
                                     start=(icx == 0), stop=False)
                        return t.matmul(pv[:, half * 512:(half + 1) * 512], c_ones_bf[0:1, 0:128],
                                        r_bv[0:1, half * 512:(half + 1) * 512], start=False, stop=True)
                    e_v = E.op("tensor", vproj,
                               after=[e_tr1ev[c], e_vcH.get((gc - 1, half)), e_vone], inc=CT)
                    e_proj_tile[j] = e_v
                    e_vcH[(gc, half)] = E.op("vector", lambda v, cp=cpar, half=half: v.tensor_copy(
                        vaug[cp][:].rearrange("p (h w) -> p h w", w=65)[:, 8 * half:8 * (half + 1), 0:64],
                        pv[:, half * 512:(half + 1) * 512].rearrange("p (hh w) -> p hh w", w=64)),
                        after=[e_v, e_kvmmh.get((gc - 2, half))], inc=CV)
                kv_pend[0] = (gc, cpar)

            for _ in range(max(1, -(-6 * ICC // NT1) - 4)):
                next(wqgen)

        for j in range(NT1 + 2):
            if 1 <= j <= NT1:
                emit_stageB1(j - 1)
            if j >= 2:
                emit_stageB2a(j - 2)
            if j < NT1:
                emit_stageA(j)
            if j >= 2:
                emit_stageB2b(j - 2)
        pgc, pcp = kv_pend[0]
        emit_kvmm_h(pgc, pcp, 0)
        emit_kvmm_h(pgc, pcp, 1)
        kv_pend[0] = None
        for _ in range(4 * ICC):
            next(wqgen)
        for _ in range(2 * DC):
            next(wqogen)

        def kvevac(v):
            v.drain()
            last = None
            for h in range(H):
                ptile = kvA if h < 8 else kvB
                g = (h % 8) // 2
                rowb = 64 * (h % 2)
                oc = h // 2
                v.tensor_copy(kvbd[rowb:rowb + 64, oc * 128 + rowb: oc * 128 + rowb + 64],
                              ptile[rowb:rowb + 64, g * 65: g * 65 + 64])
                last = v.tensor_copy(zlhs[rowb:rowb + 64, (h // 2) * 16 + h:(h // 2) * 16 + h + 1],
                                     ptile[rowb:rowb + 64, g * 65 + 64: g * 65 + 65])
            return last
        e_kvbd = E.op("vector", kvevac, after=[e_kvmmh[(NCH - 1, 0)], e_kvmmh[(NCH - 1, 1)]], inc=CV)

        E.barrier(GATE, {
            "vector": [e_kvbd],
            "scalar": [e_xnf[0], e_xnf[1], E.stream_ev(sSp[0]), E.stream_ev(sSp[1]),
                       E.stream_ev(sSp[2]), E.stream_ev(sSp2[0]), E.stream_ev(sSp2[1])],
            "tensor": [e_kvmmh[(NCH - 1, 0)], e_kvmmh[(NCH - 1, 1)]],
            "gpsimd": [E.stream_ev(sWs[0]), E.stream_ev(sWs[1]), E.stream_ev(sWs[2])],
            "sync": [],
        })
        p1.close()

        def gpamax(g):
            return g.tensor_reduce(sc_tmp[0:1, 0:1], amax_sl[:], AX.XYZWC, ALU.max)
        e_gmax = E.op("gpsimd", gpamax, after=[e_am3[0], e_am3[1], e_am3[2]], inc=CC)
        e_cin = E.dma("gpsimd", ccx_in[:, :], sc_tmp[0:1, 0:16], sCc, after=[e_gmax], serial=True)
        e_cc1 = E.op("gpsimd", lambda g: g.collective_compute(
            "AllReduce", ALU.max, replica_groups=[list(range(n_cores))],
            ins=[ccx_in[:, :]], outs=[ccx_out[:, :]]), after=[e_cin], inc=CC)
        e_cback = E.dma("gpsimd", sc_tmp[0:1, 1:2], ccx_out[0:1, 0:1], sCc, after=[e_cc1], serial=True)


        p2 = contextlib.ExitStack()
        xnfm2 = [sb(f"xnfm2_{i}", [128, DC * TS], BF16, stack=p2) for i in range(2)]
        xnf2 = [sb(f"xnf2_{i}", [128, D], F32, stack=p2) for i in range(2)]
        qb = sb("qb", [128, DC * TS], BF16, stack=p2)
        qsb = sb("qsb", [128, DC * TS], BF16, stack=p2)
        ob = sb("ob", [128, DC * TS], BF16, stack=p2)
        qa_bf = sb("qa_bf", [128, TS], BF16, stack=p2)
        qe_bf = sb("qe_bf", [128, TS], BF16, stack=p2)
        zf = sb("zf", [16, TS], F32, stack=p2)
        zbf = sb("zbf", [16, TS], BF16, stack=p2)
        ub = sb("ub", [128, D], F32, stack=p2)
        lnb_t = [sb(f"lnb{i}", [128, CPT * D], BF16, stack=p2) for i in range(2)]
        stt2 = sb("stt2", [128, 12], F32, stack=p2)
        agg2 = sb("agg2", [128, CPT * 2], F32, stack=p2)
        rstd2 = sb("rstd2", [128, CPT], F32, stack=p2)
        negmr2 = sb("negmr2", [128, CPT], F32, stack=p2)
        pq = psum("pq", [128, 2 * 512], F32, stack=p2)
        pz = psum("pz", [16, 512], F32, stack=p2)
        pzb = psum("pzb", [128, 512], F32, stack=p2)
        po = psum("po", [128, 2 * 512], F32, stack=p2)
        pat = psum("pat", [128, 1024], F32, stack=p2)


        e_l2a = [None, None]
        e_lnc = [None, None]
        e_sp2 = [None, None]
        e_qev = [None, None]
        e_oev = [None, None]
        e_uadd = None
        e_uadd_pp = [None, None]
        e_zf = None
        e_zb_last = None
        e_lnc_last = None
        e_qfin_tile = {}
        e_o_tile = {}
        e_at_tile = {}
        e_qmm_tile = {}

        for j in range(NT):
            par = j % 2
            t0 = j * TS
            e_l2a[par] = E.dma("sync",
                               xnfm2[par][:].rearrange("p (i t) -> p i t", t=TS),
                               hxnb.rearrange("(c p) s -> p c s", p=128)[:, :, t0:t0 + TS],
                               sLb[par], after=[e_qmm_tile.get(j - 2)])

            prev_qs = e_qfin_tile.get(j - 1, [None] * DC)
            prev_o = e_o_tile.get(j - 1, [None] * DC)
            qs_events = [None] * DC
            o_events = [None] * DC
            for oc in range(DC):
                qpar = oc % 2

                def qproj(t, par=par, oc=oc, qpar=qpar):
                    last = None
                    for icx in range(DC):
                        last = t.matmul(pq[:, qpar * 512:(qpar + 1) * 512],
                                        wq_bf[:, icx * D + oc * 128: icx * D + oc * 128 + 128],
                                        xnfm2[par][:, icx * TS:(icx + 1) * TS],
                                        start=(icx == 0), stop=(icx == DC - 1))
                    return last
                e_q = E.op("tensor", qproj, after=[e_l2a[par], e_qev[qpar], e_qkvo[0], e_qkvo[1]], inc=CT)
                e_qmm_tile[j] = e_q

                e_m = E.op("vector", lambda v, oc=oc, qp=qpar: v.tensor_scalar(
                    qa_bf[:], pq[:, qp * 512:(qp + 1) * 512],
                    bq_cols[:, oc:oc + 1], 0.0, ALU.add, ALU.min),
                    after=[e_q, e_qev[0], e_qev[1]], inc=CV)
                e_e = E.op("scalar", lambda s: s.activation(qe_bf[:], qa_bf[:], AF.Exp),
                           after=[e_m], inc=CS)

                def qfin(v, oc=oc, qp=qpar):
                    v.drain()
                    v.tensor_scalar(qa_bf[:], pq[:, qp * 512:(qp + 1) * 512],
                                    bq_cols[:, oc:oc + 1], 0.0, ALU.add, ALU.max)
                    v.drain()
                    return v.tensor_add(qb[:, oc * TS:(oc + 1) * TS], qa_bf[:], qe_bf[:])
                e_qev[qpar] = E.op("vector", qfin, after=[e_e, prev_qs[oc]], inc=CV)
                qs_events[oc] = e_qev[qpar]

            e_z = None
            for oc in range(DC):
                e_z = E.op("tensor", lambda t, oc=oc: t.matmul(
                    pz[:], zlhs[:, oc * 16:(oc + 1) * 16], qb[:, oc * TS:(oc + 1) * TS],
                    start=(oc == 0), stop=(oc == DC - 1)),
                    after=[qs_events[oc], e_zf], inc=CT)

            def zcalc(v):
                v.drain()
                v.tensor_scalar_add(zf[:], pz[:], Z_EPS)
                v.drain()
                v.reciprocal(zf[:], zf[:])
                v.drain()
                return v.tensor_copy(zbf[:], zf[:])
            e_zf = E.op("vector", zcalc, after=[e_z, e_zb_last], inc=CV)

            e_zbev = None
            for oc in range(DC):
                opar = oc % 2
                e_zb = E.op("tensor", lambda t, oc=oc: t.matmul(
                    pzb[:, :], c_sel[:, oc * 128:(oc + 1) * 128], zbf[:, :],
                    start=True, stop=True),
                    after=[e_zf, e_zbev], inc=CT)
                e_zb_last = e_zb

                def qsmul(v, oc=oc):
                    return v.tensor_tensor(qsb[:, oc * TS:(oc + 1) * TS],
                                           qb[:, oc * TS:(oc + 1) * TS], pzb[:, :], ALU.mult)
                e_zbev = E.op("vector", qsmul, after=[e_zb, prev_o[oc]], inc=CV)
                qs_events[oc] = e_zbev

                e_o = E.op("tensor", lambda t, oc=oc, op=opar: t.matmul(
                    po[:, op * 512:(op + 1) * 512], kvbd[:, oc * 128:(oc + 1) * 128],
                    qsb[:, oc * TS:(oc + 1) * TS], start=True, stop=True),
                    after=[e_zbev, e_oev[opar]], inc=CT)
                o_events[oc] = e_o
                e_oev[opar] = E.op("vector", lambda v, oc=oc, op=opar: v.tensor_copy(
                    ob[:, oc * TS:(oc + 1) * TS], po[:, op * 512:(op + 1) * 512]),
                    after=[e_o, e_at_tile.get(j - 1)], inc=CV)
            e_qfin_tile[j] = qs_events
            e_o_tile[j] = o_events

            for c in range(CPT):
                def atmm(t, c=c):
                    last = None
                    for icx in range(DC):
                        lhs = ob[:, icx * TS + c * 128: icx * TS + (c + 1) * 128]
                        for half in range(2):
                            t.matmul(pat[:, half * 512:(half + 1) * 512], lhs,
                                     wo_bf[:, icx * D + half * 512: icx * D + (half + 1) * 512],
                                     start=(icx == 0), stop=False)
                    for half in range(2):
                        last = t.matmul(pat[:, half * 512:(half + 1) * 512], c_ones_bf[0:1, 0:128],
                                        r_bo[0:1, half * 512:(half + 1) * 512], start=False, stop=True)
                    return last
                e_at = E.op("tensor", atmm, after=[e_oev[0], e_oev[1], e_uadd, e_qkvo[0], e_qkvo[1]], inc=CT)
                e_at_tile[j] = e_at

                cpar2 = (j * CPT + c) % 2
                e_xn2 = E.dma("sync", xnf2[cpar2][:],
                              tm_view(hxnf, t0 + c * 128, 128), sL[cpar2],
                              after=[e_uadd_pp[cpar2]])
                e_uadd = E.op("vector", lambda v, cp=cpar2: v.tensor_add(
                    ub[:], pat[:], xnf2[cp][:]),
                    after=[e_at, e_xn2, e_lnc_last], inc=CV)
                e_uadd_pp[cpar2] = e_uadd

                def stats2(v, c=c):
                    v.drain()
                    v.bn_stats(stt2[:, 0:6], ub[:, 0:512])
                    v.bn_stats(stt2[:, 6:12], ub[:, 512:1024])
                    v.drain()
                    v.bn_aggr(agg2[:, 2 * c:2 * c + 2], stt2[:])
                    v.drain()
                    return v.tensor_scalar_add(rstd2[:, c:c + 1],
                                               agg2[:, 2 * c + 1:2 * c + 2], LN_EPS)
                e_s2 = E.op("vector", stats2, after=[e_uadd], inc=CV)
                e_sq2 = E.op("scalar", lambda s, c=c: s.activation(
                    rstd2[:, c:c + 1], rstd2[:, c:c + 1], AF.Sqrt), after=[e_s2], inc=CS)

                def fin2(v, c=c):
                    v.reciprocal(rstd2[:, c:c + 1], rstd2[:, c:c + 1])
                    v.drain()
                    v.tensor_tensor(negmr2[:, c:c + 1], agg2[:, 2 * c:2 * c + 1],
                                    rstd2[:, c:c + 1], ALU.mult)
                    v.drain()
                    return v.tensor_scalar_mul(negmr2[:, c:c + 1], negmr2[:, c:c + 1], -1.0)
                e_f2 = E.op("vector", fin2, after=[e_sq2], inc=CV)
                e_lnc[par] = E.op("scalar", lambda s, par=par, c=c: s.activation(
                    lnb_t[par][:, c * D:(c + 1) * D], ub[:], AF.Identity,
                    bias=negmr2[:, c:c + 1], scale=rstd2[:, c:c + 1]),
                    after=[e_f2, e_sp2[par] if c == 0 else None], inc=CS)
                e_lnc_last = e_lnc[par]
            e_sp2[par] = E.dma("scalar", tm_view(hlnc1, t0, TS), lnb_t[par][:],
                               sSp[par], after=[e_lnc[par]])

        E.barrier(GATE, {
            "vector": [e_zf, e_oev[0], e_oev[1], e_uadd],
            "scalar": [e_lnc[0], e_lnc[1], E.stream_ev(sSp[0]), E.stream_ev(sSp[1])],
            "tensor": [e_at_tile[NT - 1]],
            "gpsimd": [E.stream_ev(sWs[0]), E.stream_ev(sWs[1])],
            "sync": [],
        })
        p2.close()
        s_qkvo.close()

        p3 = contextlib.ExitStack()
        xnf3 = [sb(f"xnf3_{i}", [128, CPT * D], F32, stack=p3) for i in range(2)]
        rb_t = [sb(f"rb{i}", [128, CPT * D], BF16, stack=p3) for i in range(2)]
        rfm = [sb(f"rfm{i}", [128, DC * TS], BF16, stack=p3) for i in range(2)]
        rt1 = sb("rt1", [128, D], F32, stack=p3)
        rt2 = sb("rt2", [128, D], F32, stack=p3)
        wgqT = sb("wgqT", [128, DC * I_FF], BF16, stack=p3)
        wuqT = sb("wuqT", [128, DC * I_FF], BF16, stack=p3)
        sgb = [sb(f"sgb{i}", [128, 512], BF16, stack=p3) for i in range(2)]
        pbuf = [sb(f"pbuf{i}", [128, 512], F32, stack=p3) for i in range(2)]
        pg = psum("pg", [128, 2 * 512], F32, stack=p3)
        pu = psum("pu", [128, 2 * 512], F32, stack=p3)
        ptr3 = psum("ptr3", [128, 1024], BF16, stack=p3)
        pcol3 = psum("pcol3", [1, 512], F32, stack=p3)
        pbx3 = psum("pbx3", [128, 512], F32, stack=p3)

        IHH = I_FF // 2
        wdst = sb("wdst", [128, I_FF], F32, stack=p3)
        wdout = sb("wdout", [128, IHH], BF16, stack=p3)
        wdsc = sb("wdsc", [128, 2], F32, stack=p3)

        def wd_steps():
            prev_q = None
            prev_mul = None
            for ci in range(DC):
                e_l = E.dma("gpsimd", wdst[:], dwd[ci * 128:(ci + 1) * 128, :],
                            sW[ci % 2], after=[prev_mul])

                def quant(v):
                    v.drain()
                    v.tensor_reduce(wdsc[:, 0:1], wdst[:], AX.X, ALU.max,
                                    apply_absolute_value=True)
                    v.drain()
                    v.tensor_scalar_mul(wdsc[:, 0:1], wdsc[:, 0:1], 1.0 / 127.0)
                    v.drain()
                    v.reciprocal(wdsc[:, 1:2], wdsc[:, 0:1])
                    v.drain()
                    return v.tensor_scalar(wdst[:], wdst[:], wdsc[:, 1:2],
                                           MAGIC, ALU.mult, ALU.add)
                e_q = E.op("vector", quant, after=[e_l], inc=CV)
                for h in range(2):
                    e_t2 = E.op("vector", lambda v, h=h: v.tensor_scalar(
                        wdout[:], wdst[:, h * IHH:(h + 1) * IHH], MAGIC, wdsc[:, 0:1],
                        ALU.subtract, ALU.mult), after=[e_q, prev_q], inc=CV)
                    prev_mul = e_t2
                    prev_q = E.dma("gpsimd", hwdq[ci * 128:(ci + 1) * 128, h * IHH:(h + 1) * IHH],
                                   wdout[:], sWs[ci % 2], after=[e_t2])
                yield
            while True:
                yield
        wdgen = wd_steps()


        def sxcalc(v):
            v.drain()
            v.tensor_tensor(sc_tmp[0:1, 2:3], sc_tmp[0:1, 1:2], sc_clip[0:1, 0:1], ALU.mult)
            v.drain()
            v.tensor_scalar_mul(sc_tmp[0:1, 2:3], sc_tmp[0:1, 2:3], 1.0 / 127.0)
            v.drain()
            return v.reciprocal(sc_tmp[0:1, 3:4], sc_tmp[0:1, 2:3])
        e_sxg = E.op("vector", sxcalc, after=[e_cback, e_sig], inc=CV)

        e_bxx = E.op("tensor", lambda t: t.matmul(pbx3[:, 0:2], c_ones_f[0:1, 0:128],
                                                  sc_tmp[0:1, 2:4], start=True, stop=True),
                     after=[e_sxg], inc=CT)

        def sxevac(v):
            v.tensor_copy(sc_sx[:], pbx3[:, 0:1])
            return v.tensor_copy(sc_rsx[:], pbx3[:, 1:2])
        e_sxb = E.op("vector", sxevac, after=[e_bxx], inc=CV)

        wq_sp_done0 = E.stream_ev(sWs[0])
        wq_sp_done1 = E.stream_ev(sWs[1])
        wq_sp_done2 = E.stream_ev(sWs[2])
        e_l3_pre = {}
        for jp in range(min(2, NT)):
            e_l3_pre[jp] = E.dma("sync", xnf3[jp % 2][:], tm_view(hxnf, jp * TS, TS),
                                 sL[jp % 2])

        IH = 1408
        e_wT_half = [None, None]
        for half in range(2):
            for wtile, hsrc in [(wgqT, hwgq), (wuqT, hwuq)]:
                for icx in range(DC):
                    e_wT_half[half] = E.dma(
                        "sync", wtile[:, icx * I_FF + half * IH: icx * I_FF + (half + 1) * IH],
                        hsrc[half * IH:(half + 1) * IH, icx * 128:(icx + 1) * 128],
                        sWt if half == 0 else sWt2,
                        after=[wq_sp_done0, wq_sp_done1, wq_sp_done2], transpose=True)
        e_wTa = e_wT_half[0]
        e_wTb = e_wT_half[1]

        e_l3 = [None, None]
        e_rdone = [None, None]
        e_rsp = [None, None]
        e_rfm = [None, None]
        e_pev = [None, None]
        e_sg = [None, None]
        e_pspill = {}
        e_gu_tile = {}

        for j in range(NT):
            par = j % 2
            t0 = j * TS
            if j in e_l3_pre:
                e_l3[par] = e_l3_pre[j]
            else:
                e_l3[par] = E.dma("sync", xnf3[par][:], tm_view(hxnf, t0, TS), sL[par],
                                  after=[e_rdone[par]])

            def rcalc(v, par=par):
                last = None
                for c in range(CPT):
                    xin = xnf3[par][:, c * D:(c + 1) * D]
                    v.tensor_scalar(rt1[:], xin, sc_rsx[:, 0:1], MAGIC, ALU.mult, ALU.add)
                    v.drain()
                    v.tensor_scalar(rt2[:], rt1[:], MAGIC, 127.0, ALU.subtract, ALU.min)
                    v.drain()
                    last = v.tensor_scalar_max(rb_t[par][:, c * D:(c + 1) * D], rt2[:], -128.0)
                    v.drain()
                return last
            e_rdone[par] = E.op("vector", rcalc, after=[e_l3[par], e_sxb, e_rfm[par]], inc=CV)
            e_tr3e = None
            for c in range(CPT):
                def trp(t, par=par, c=c):
                    last = None
                    for icx in range(DC):
                        last = t.transpose(ptr3[:, icx * 128:(icx + 1) * 128],
                                           rb_t[par][:, c * D + icx * 128: c * D + (icx + 1) * 128],
                                           c_id_bf[:])
                    return last
                e_trm = E.op("tensor", trp, after=[e_rdone[par], e_tr3e,
                                                  e_gu_tile.get(j - 2) if c == 0 else None], inc=CT)
                def trev(v, par=par, c=c):
                    return v.tensor_copy(
                        rfm[par][:].rearrange("p (i t) -> p i t", t=TS)[:, :, c * 128:(c + 1) * 128],
                        ptr3[:].rearrange("p (i w) -> p i w", w=128))
                e_tr3e = E.op("vector", trev, after=[e_trm], inc=CV)
            e_rfm[par] = e_tr3e

            for oci in range(ICC):
                gpar = oci % 2

                def gumm(t, par, oci, gpar, which):
                    last = None
                    wsrc = wgqT if which == 0 else wuqT
                    pdst = pg if which == 0 else pu
                    for icx in range(DC):
                        last = t.matmul(pdst[:, gpar * 512:(gpar + 1) * 512],
                                        wsrc[:, icx * I_FF + oci * 128: icx * I_FF + oci * 128 + 128],
                                        rfm[par][:, icx * TS:(icx + 1) * TS],
                                        start=(icx == 0), stop=(icx == DC - 1))
                    return last
                e_g = E.op("tensor", lambda t, par=par, oci=oci, gp=gpar: gumm(t, par, oci, gp, 0),
                           after=[e_rfm[par], e_wT_half[0] if oci < IH // 128 else e_wT_half[1],
                                  e_sg[gpar]], inc=CT)
                e_u = E.op("tensor", lambda t, par=par, oci=oci, gp=gpar: gumm(t, par, oci, gp, 1),
                           after=[e_pev[gpar], e_wT_half[0] if oci < IH // 128 else e_wT_half[1]], inc=CT)
                e_gu_tile[j] = e_u
                e_sg[gpar] = E.op("scalar", lambda s, gp=gpar: s.activation(
                    sgb[gp][:], pg[:, gp * 512:(gp + 1) * 512], AF.Silu,
                    scale=sc_sx[:, 0:1]), after=[e_g, e_sxb], inc=CS)

                def pmul(v, gp=gpar, j=j, oci=oci):
                    v.scalar_tensor_tensor(pbuf[gp][:], pu[:, gp * 512:(gp + 1) * 512],
                                           sc_sx[:, 0:1], sgb[gp][:], ALU.mult, ALU.mult)
                    v.drain()
                    return v.tensor_reduce(pmax_sl[:, j * ICC + oci: j * ICC + oci + 1],
                                           pbuf[gp][:], AX.X, ALU.max, apply_absolute_value=True)
                prev_sp = (e_pspill.get((j, oci - 2)) if oci >= 2
                           else e_pspill.get((j - 1, ((ICC - 1 - oci) // 2) * 2 + oci)))
                e_pev[gpar] = E.op("vector", pmul, after=[e_u, e_sg[gpar], prev_sp], inc=CV)
                e_pspill[(j, oci)] = E.dma("scalar", hp[oci * 128:(oci + 1) * 128, t0:t0 + TS],
                                           pbuf[gpar][:], sSp2[gpar], after=[e_pev[gpar]])
            next(wdgen)

        for _ in range(DC):
            next(wdgen)

        if dbg:
            E.dma("gpsimd", hsc[:, :], sc_tmp[0:1, 0:16], sCc, after=[e_pev[0], e_pev[1]], serial=True)
            E.dma("gpsimd", hrfm[:, :], rfm[0][:], sCc, after=[e_gu_tile.get(NT - 1)], serial=True)
        E.barrier(GATE, {
            "vector": [e_pev[0], e_pev[1], e_sxb],
            "scalar": [e_sg[0], e_sg[1], E.stream_ev(sSp2[0]), E.stream_ev(sSp2[1]),
                       E.stream_ev(sSp[0]), E.stream_ev(sSp[1])],
            "tensor": [e_gu_tile[NT - 1], e_bxx],
            "gpsimd": [],
            "sync": [],
        })
        p3.close()

        def gpdmax(g):
            return g.tensor_reduce(sc_tmp[0:1, 4:5], pmax_sl[:], AX.XYZWC, ALU.max)
        e_pmax = E.op("gpsimd", gpdmax, after=[e_pev[0], e_pev[1]], inc=CC)
        e_cin2 = E.dma("gpsimd", ccd_in[:, :], sc_tmp[0:1, 0:16], sCc, after=[e_pmax], serial=True)
        e_cc2 = E.op("gpsimd", lambda g: g.collective_compute(
            "AllReduce", ALU.max, replica_groups=[list(range(n_cores))],
            ins=[ccd_in[:, :]], outs=[ccd_out[:, :]]), after=[e_cin2], inc=CC)
        e_cback2 = E.dma("gpsimd", sc_tmp[0:1, 5:6], ccd_out[0:1, 4:5], sCc, after=[e_cc2], serial=True)


        p4 = contextlib.ExitStack()
        w1b = sb("w1b", [128, DC * FF], BF16, stack=p4)
        w2b = sb("w2b", [128, FFC * D], BF16, stack=p4)
        r_b1p = sb("r_b1p", [1, FF], BF16, stack=p4)
        p4pro = contextlib.ExitStack()
        w4st = [sb(f"w4st{i}", [128, FF], F32, stack=p4pro) for i in range(2)]
        w4st3 = sb("w4st3", [128, D], F32, stack=p4pro)
        b1cols = sb("b1cols", [128, FFC], F32, stack=p4pro)
        b1cols_bf = sb("b1cols_bf", [128, FFC], BF16, stack=p4pro)
        b1stg = sb("b1stg", [FFC, 128], F32, stack=p4pro)
        b1tr = sb("b1tr", [FFC, 128], BF16, stack=p4pro)
        pb1 = psum("pb1", [128, 512], F32, stack=p4pro)
        pb1T = psum("pb1T", [128, 512], F32, stack=p4pro)
        pb1bf = psum("pb1bf", [FFC, 1024], BF16, stack=p4pro)
        e_b1stg = E.dma("sync", b1stg[:], db1.rearrange("(c p) -> c p", p=128), sMisc, serial=True)
        e_b1tr = E.op("tensor", lambda t: t.transpose(pb1T[:, 0:FFC], b1stg[:], c_id[0:FFC, 0:FFC]),
                      after=[e_b1stg], inc=CT)
        e_b1raw = E.op("vector", lambda v: v.tensor_copy(b1cols[:], pb1T[:, 0:FFC]),
                       after=[e_b1tr], inc=CV)

        e_w1c = [None, None, None]
        e_b1mm = None
        for icx in range(DC):
            par = icx % 2
            e_lw1 = E.dma("sync" if par == 0 else "scalar",
                          w4st[par][:], dw1[icx * 128:(icx + 1) * 128, :], sW4[par],
                          after=[e_w1c[par]])

            def b1mms(t, icx=icx, par=par):
                last = None
                for oc in range(FFC):
                    last = t.matmul(pb1[:, oc:oc + 1],
                                    w4st[par][:, oc * 128:(oc + 1) * 128],
                                    be1cols[:, icx:icx + 1],
                                    start=(icx == 0 and oc == 0),
                                    stop=(icx == DC - 1 and oc == FFC - 1),
                                    skip_group_check=True)
                return last
            e_b1mm = E.op("tensor", b1mms, after=[e_lw1, e_b1mm], inc=CT)
            e_w1c[par] = E.op("vector", lambda v, icx=icx, par=par: v.tensor_scalar_mul(
                w1b[:, icx * FF:(icx + 1) * FF], w4st[par][:], g1cols[:, icx:icx + 1]),
                after=[e_lw1, e_b1mm], inc=CV)

        def b1fin(v):
            v.drain()
            v.tensor_add(b1cols[:], b1cols[:], pb1[:, 0:FFC])
            v.drain()
            return v.tensor_copy(b1cols_bf[:], b1cols[:])
        e_b1e = E.op("vector", b1fin, after=[e_b1mm, e_b1raw], inc=CV)
        e_b1t2 = E.op("tensor", lambda t: t.transpose(pb1bf[:, 0:128], b1cols_bf[:],
                                                      c_id_bf[:]), after=[e_b1e], inc=CT)
        e_b1e2 = E.op("vector", lambda v: v.tensor_copy(b1tr[:], pb1bf[:, 0:128]),
                      after=[e_b1t2], inc=CV)
        e_b1sp = E.dma("scalar", hb1p.rearrange("(c p) -> c p", p=128), b1tr[:],
                       sMisc, after=[e_b1e2], serial=True)
        e_b1r = E.dma("sync", r_b1p[:], hb1p[None, :], sMisc, after=[e_b1sp], serial=True)

        e_w2c = [None, None, None]
        for icx in range(FFC):
            par = icx % 3
            stg = w4st3 if par == 2 else w4st[par]
            e_lw2 = E.dma(("sync", "scalar", "gpsimd")[par],
                          stg[:, 0:D], dw2[icx * 128:(icx + 1) * 128, :], sW4[par],
                          after=[e_w2c[par], e_b1sp if icx < 3 else None])
            e_w2c[par] = E.op("vector", lambda v, icx=icx, stg=stg: v.tensor_copy(
                w2b[:, icx * D:(icx + 1) * D], stg[:, 0:D]), after=[e_lw2], inc=CV)

        p4pro.close()

        lnfm = [sb(f"lnfm{i}", [128, DC * TS4], BF16, stack=p4) for i in range(2)]
        lntm = [sb("lntm0", [128, CPT4 * D], BF16, stack=p4)] * 2
        h1b = sb("h1b", [128, FFC * TS4], BF16, stack=p4)
        r2b = sb("r2b", [128, D], F32, stack=p4)
        sbuf_s = [sb("sbs0", [128, CPT4 * D], F32, stack=p4)] * 2
        stt4 = sb("stt4", [128, 12], F32, stack=p4)
        agg4 = sb("agg4", [128, CPT4 * 2], F32, stack=p4)
        rstd4 = sb("rstd4", [128, CPT4], F32, stack=p4)
        negmr4 = sb("negmr4", [128, CPT4], F32, stack=p4)
        ph1 = psum("ph1", [128, 2 * 1024], F32, stack=p4)
        py = psum("py", [128, 1024], F32, stack=p4)
        ptr4 = psum("ptr4", [128, 1024], BF16, stack=p4)

        e_l4a = [None, None]
        e_l4b = [None, None]
        e_h1ev = [None, None]
        e_ymm = [None, None]
        e_sdone = [None, None]
        e_ssp = [None, None]
        e_r2 = None
        e_sdone_last = None
        e_h1mm_tile = {}
        e_ymm_tile = {}
        e_r2_tile = {}
        e_gel_tile = {}

        for j4 in range(NT4):
            par = j4 % 2
            t0 = j4 * TS4
            pro_done = [e_w2c[0], e_w2c[1], e_w2c[2]] if j4 < 2 else []
            e_l4b[par] = E.dma("sync", lntm[par][:], tm_view(hlnc1, t0, TS4), sL[par],
                               after=[e_r2_tile.get(j4 - 1)] + pro_done)
            e_tr4e = None
            for c in range(CPT4):
                def trp4(t, par=par, c=c):
                    last = None
                    for icx in range(DC):
                        last = t.transpose(ptr4[:, icx * 128:(icx + 1) * 128],
                                           lntm[par][:, c * D + icx * 128: c * D + (icx + 1) * 128],
                                           c_id_bf[:])
                    return last
                e_trm4 = E.op("tensor", trp4, after=[e_l4b[par], e_tr4e,
                                                    e_h1mm_tile.get(j4 - 2) if c == 0 else None], inc=CT)
                def trev4(v, par=par, c=c):
                    return v.tensor_copy(
                        lnfm[par][:].rearrange("p (i t) -> p i t", t=TS4)[:, :, c * 128:(c + 1) * 128],
                        ptr4[:].rearrange("p (i w) -> p i w", w=128))
                e_tr4e = E.op("vector", trev4, after=[e_trm4], inc=CV)
            e_l4a[par] = e_tr4e

            for grp in range(FFC // 4):
                gpar = grp % 2

                def h1mm(t, par=par, grp=grp, gp=gpar):
                    last = None
                    for oci in range(4):
                        oc = grp * 4 + oci
                        dst = ph1[:, gp * 1024 + oci * 256: gp * 1024 + (oci + 1) * 256]
                        for icx in range(DC):
                            t.matmul(dst,
                                     w1b[:, icx * FF + oc * 128: icx * FF + oc * 128 + 128],
                                     lnfm[par][:, icx * TS4:(icx + 1) * TS4],
                                     start=(icx == 0), stop=False)
                        last = t.matmul(dst, r_b1p[0:1, oc * 128:(oc + 1) * 128],
                                        c_ones_bf[0:1, 0:TS4], start=False, stop=True)
                    return last
                e_h1 = E.op("tensor", h1mm, after=[e_l4a[par], e_w1c[0], e_w1c[1], e_w1c[2], e_b1r,
                                                   e_h1ev[gpar]], inc=CT)
                e_h1mm_tile[j4] = e_h1
                e_h1ev[gpar] = E.op("scalar", lambda s, grp=grp, gp=gpar: s.activation(
                    h1b[:, grp * 4 * TS4:(grp + 1) * 4 * TS4],
                    ph1[:, gp * 1024:(gp + 1) * 1024], AF.Gelu),
                    after=[e_h1, e_ymm_tile.get(j4 - 1)], inc=CS)
                e_gel_tile[j4] = e_h1ev[gpar]

            for c in range(CPT4):
                ypar = c % 2

                def ymm(t, c=c):
                    last = None
                    for icx in range(FFC):
                        lhs = h1b[:, icx * TS4 + c * 128: icx * TS4 + (c + 1) * 128]
                        for half in range(2):
                            t.matmul(py[:, half * 512:(half + 1) * 512],
                                     lhs, w2b[:, icx * D + half * 512: icx * D + (half + 1) * 512],
                                     start=(icx == 0), stop=False)
                    for half in range(2):
                        last = t.matmul(py[:, half * 512:(half + 1) * 512],
                                        c_ones_bf[0:1, 0:128],
                                        r_b2pp[0:1, half * 512:(half + 1) * 512],
                                        start=False, stop=True)
                    return last
                e_y = E.op("tensor", ymm, after=[e_gel_tile.get(j4), e_w2c[0], e_w2c[1], e_w2c[2],
                                                 e_ymm[0], e_ymm[1]], inc=CT)
                e_ymm_tile[j4] = e_y

                def r2calc(v, par=par, c=c):
                    v.drain()
                    v.tensor_tensor(r2b[:], lntm[par][:, c * D:(c + 1) * D], g1b[:], ALU.mult)
                    v.drain()
                    return v.tensor_add(r2b[:], r2b[:], py[:, 0:1024])
                e_r2 = E.op("vector", r2calc, after=[e_y, e_l4b[par], e_sdone_last], inc=CV)
                e_r2_tile[j4] = e_r2
                e_ymm[ypar] = e_r2

                def stats4(v, c=c):
                    v.drain()
                    v.bn_stats(stt4[:, 0:6], r2b[:, 0:512])
                    v.bn_stats(stt4[:, 6:12], r2b[:, 512:1024])
                    v.drain()
                    v.bn_aggr(agg4[:, 2 * c:2 * c + 2], stt4[:])
                    v.drain()
                    return v.tensor_scalar_add(rstd4[:, c:c + 1], agg4[:, 2 * c + 1:2 * c + 2],
                                               LN_EPS)
                e_s4 = E.op("vector", stats4, after=[e_r2], inc=CV)
                e_sq4 = E.op("scalar", lambda s, c=c: s.activation(
                    rstd4[:, c:c + 1], rstd4[:, c:c + 1], AF.Sqrt), after=[e_s4], inc=CS)

                def fin4(v, c=c):
                    v.reciprocal(rstd4[:, c:c + 1], rstd4[:, c:c + 1])
                    v.drain()
                    v.tensor_tensor(negmr4[:, c:c + 1], agg4[:, 2 * c:2 * c + 1],
                                    rstd4[:, c:c + 1], ALU.mult)
                    v.drain()
                    return v.tensor_scalar_mul(negmr4[:, c:c + 1], negmr4[:, c:c + 1], -1.0)
                e_f4 = E.op("vector", fin4, after=[e_sq4], inc=CV)
                e_lnc2 = E.op("scalar", lambda s, c=c: s.activation(
                    r2b[:], r2b[:], AF.Identity, bias=negmr4[:, c:c + 1], scale=rstd4[:, c:c + 1]),
                    after=[e_f4, e_sdone_last], inc=CS)

                def scalc(v, par=par, c=c):
                    v.drain()
                    slot = sbuf_s[par][:, c * D:(c + 1) * D]
                    v.tensor_tensor(slot, r2b[:], g2b[:], ALU.mult)
                    v.drain()
                    return v.tensor_add(slot, slot, be2b[:])
                e_sdone[par] = E.op("vector", scalc,
                                    after=[e_lnc2, e_ssp[0] if c == 0 else None,
                                           e_ssp[1] if c == 0 else None], inc=CV)
                e_sdone_last = e_sdone[par]
            e_ssp[par] = E.dma("scalar", tm_view(hs, t0, TS4), sbuf_s[par][:],
                               sSp[par], after=[e_sdone[par]])

        E.barrier(GATE, {
            "vector": [e_sdone[0], e_sdone[1]],
            "scalar": [E.stream_ev(sSp[0]), E.stream_ev(sSp[1])],
            "tensor": [e_ymm_tile[NT4 - 1]],
            "gpsimd": [],
            "sync": [],
        })
        p4.close()

        p5 = contextlib.ExitStack()
        wdqT = sb("wdqT", [128, ICC * D], BF16, stack=p5)
        pfb = [sb(f"pfb{i}", [128, TS], F32, stack=p5) for i in range(6)]
        rt5b = sb("rt5b", [128, TS], F32, stack=p5)
        rhb = [sb(f"rhb{i}", [128, ICC * TS], BF16, stack=p5) for i in range(2)]
        sld = [sb(f"sld{i}", [128, D], F32, stack=p5) for i in range(2)]
        xld = [sb(f"xld{i}", [128, D], F32, stack=p5) for i in range(2)]
        gbuf = sb("gbuf", [128, D], F32, stack=p5)
        tbuf = sb("tbuf", [128, D], F32, stack=p5)
        obuf = [sb(f"obuf{i}", [128, D], F32, stack=p5) for i in range(2)]
        rt5 = sb("rt5", [128, TS], F32, stack=p5)
        pd = psum("pd", [128, 2 * 1024], F32, stack=p5)
        pbx5 = psum("pbx5", [128, 512], F32, stack=p5)

        def sdcalc(v):
            v.drain()
            v.tensor_tensor(sc_tmp[0:1, 6:7], sc_tmp[0:1, 5:6], sc_clip[0:1, 1:2], ALU.mult)
            v.drain()
            v.tensor_scalar_mul(sc_tmp[0:1, 6:7], sc_tmp[0:1, 6:7], 1.0 / 127.0)
            v.drain()
            return v.reciprocal(sc_tmp[0:1, 7:8], sc_tmp[0:1, 6:7])
        e_sdg = E.op("vector", sdcalc, after=[e_cback2], inc=CV)

        e_bx5 = E.op("tensor", lambda t: t.matmul(pbx5[:, 0:2], c_ones_f[0:1, 0:128],
                                                  sc_tmp[0:1, 6:8], start=True, stop=True),
                     after=[e_sdg], inc=CT)

        def sdevac(v):
            v.tensor_copy(sc_sd[:], pbx5[:, 0:1])
            return v.tensor_copy(sc_rsd[:], pbx5[:, 1:2])
        e_sdb = E.op("vector", sdevac, after=[e_bx5], inc=CV)

        e_pl_pre = {}
        for gci in range(6):
            e_pl_pre[gci] = E.dma("sync", pfb[gci % 6][:],
                                  hp[gci * 128:(gci + 1) * 128, 0:TS], sPf[gci % 6])

        e_wdT = None
        wd_sp_done0 = E.stream_ev(sWs[0])
        wd_sp_done1 = E.stream_ev(sWs[1])
        for ici in range(ICC):
            e_wdT = E.dma("sync", wdqT[:, ici * D:(ici + 1) * D],
                          hwdq[:, ici * 128:(ici + 1) * 128], sWt,
                          after=[wd_sp_done0, wd_sp_done1], transpose=True)

        e_pl = [None, None]
        e_rh = {}
        e_a1 = {}
        e_a2 = {}
        e_clip = {}
        e_dmm = [None, None]
        e_ost = [None, None]
        e_out_last = None
        e_outv_tile = {}
        e_dmm_tile = {}
        e_sxl = [None, None]

        def emit_rh(j):
            t0 = j * TS
            for ici in range(ICC):
                gci = j * ICC + ici
                ppar = gci % 6
                p2par = gci % 2
                prev_a1 = e_a1.get(gci - 6)
                if gci < 6:
                    e_pl6 = e_pl_pre[gci]
                else:
                    e_pl6 = E.dma("sync", pfb[ppar][:],
                                  hp[ici * 128:(ici + 1) * 128, t0:t0 + TS], sPf[ppar],
                                  after=[prev_a1])
                e_a1[gci] = E.op("scalar", lambda s, pp=ppar, p2=p2par: s.activation(
                    (rt5 if p2 == 0 else rt5b)[:], pfb[pp][:], AF.Identity,
                    scale=sc_rsd[:, 0:1], bias=c_magic[:, 0:1]),
                    after=[e_pl6, e_sdb, e_clip.get(gci - 2)], inc=CS)
                e_a2[gci] = E.op("vector", lambda v, p2=p2par: v.tensor_scalar_add(
                    (rt5 if p2 == 0 else rt5b)[:], (rt5 if p2 == 0 else rt5b)[:], -MAGIC),
                    after=[e_a1[gci], e_clip.get(gci - 2)], inc=CV)
                e_clip[gci] = E.op("vector", lambda v, j=j, ici=ici, p2=p2par: v.tensor_scalar(
                    rhb[j % 2][:, ici * TS:(ici + 1) * TS], (rt5 if p2 == 0 else rt5b)[:],
                    127.0, -128.0, ALU.min, ALU.max),
                    after=[e_a2[gci], e_clip.get(gci - 1),
                           e_dmm_tile.get(j - 2) if ici == 0 else None], inc=CV)
                e_rh[(j, ici)] = e_clip[gci]

        def emit_down(j):
            t0 = j * TS
            par = j % 2
            for c in range(CPT):
                dpar = c % 2

                def dmm(t, par=par, c=c, dp=dpar):
                    last = None
                    for ici in range(ICC):
                        lhs = rhb[par][:, ici * TS + c * 128: ici * TS + (c + 1) * 128]
                        for half in range(2):
                            last = t.matmul(
                                pd[:, dp * 1024 + half * 512: dp * 1024 + (half + 1) * 512],
                                lhs, wdqT[:, ici * D + half * 512: ici * D + (half + 1) * 512],
                                start=(ici == 0), stop=(ici == ICC - 1))
                    return last
                e_d = E.op("tensor", dmm, after=[e_rh[(j, ICC - 1)], e_wdT, e_dmm[dpar]], inc=CT)
                e_dmm_tile[j] = e_d
                e_gd = E.op("scalar", lambda s, dp=dpar: s.activation(
                    gbuf[:], pd[:, dp * 1024:(dp + 1) * 1024], AF.Silu, scale=sc_sd[:, 0:1]),
                    after=[e_d, e_sdb, e_out_last2[0]], inc=CS)
                gc5 = j * CPT + c
                cp5 = gc5 % 2
                e_sl = E.dma("sync", sld[cp5][:], tm_view(hs, t0 + c * 128, 128), sL[cp5],
                             after=[e_sxl[cp5]])
                e_xl = E.dma("sync", xld[cp5][:], tm_view(dx, t0 + c * 128, 128), sLc[cp5],
                             after=[e_sxl[cp5]])

                def outcalc(v, cp=cp5):
                    v.drain()
                    v.tensor_tensor(tbuf[:], sld[cp][:], gbuf[:], ALU.mult)
                    v.drain()
                    return v.tensor_add(obuf[cp][:], tbuf[:], xld[cp][:])
                e_out = E.op("vector", outcalc, after=[e_gd, e_sl, e_xl, e_ost[cp5]], inc=CV)
                e_out_last2[0] = e_out
                e_outv_tile[j] = e_out
                e_dmm[dpar] = e_out
                e_sxl[cp5] = e_out
                e_ost[cp5] = E.dma("scalar", tm_view(dout, t0 + c * 128, 128), obuf[cp5][:],
                                   sOut[cp5], after=[e_out])

        e_out_last2 = [None]
        for j in range(NT + 1):
            if j < NT:
                emit_rh(j)
            if j >= 1:
                emit_down(j - 1)

        E.steps["gpsimd"].append(lambda e, v0=None: e.wait_ge(sOut[0], E.vals[sOut[0].name]))
        E.steps["gpsimd"].append(lambda e: e.wait_ge(sOut[1], E.vals[sOut[1].name]))

        p5.close()
        E.replay(blk)

    return nc


_NC_CACHE = {}


def kernel(**inputs):
    S = inputs["L0_attn_src"].shape[1]
    if S not in _NC_CACHE:
        _NC_CACHE[S] = build(S=S)
    nc = _NC_CACHE[S]
    x = np.ascontiguousarray(inputs["L0_attn_src"], dtype=np.float32)
    clip2 = np.stack([np.asarray(inputs["L0_ffn_clip_x"]).reshape(-1)[0],
                      np.asarray(inputs["L0_ffn_clip_down"]).reshape(-1)[0]]).reshape(1, 2).astype(np.float32)
    base = {
        "wq": inputs["L0_attn_Wq"], "wk": inputs["L0_attn_Wk"], "wv": inputs["L0_attn_Wv"],
        "wo": inputs["L0_attn_Wo"], "bq": inputs["L0_attn_bq"], "bk": inputs["L0_attn_bk"],
        "bv": inputs["L0_attn_bv"], "bo": inputs["L0_attn_bo"], "w1": inputs["L0_attn_W1"],
        "b1": inputs["L0_attn_b1"], "w2": inputs["L0_attn_W2"], "b2": inputs["L0_attn_b2"],
        "g1": inputs["L0_attn_g1"], "be1": inputs["L0_attn_be1"], "g2": inputs["L0_attn_g2"],
        "be2": inputs["L0_attn_be2"], "clip2": clip2,
        "wg": inputs["L0_ffn_w_gate"], "wu": inputs["L0_ffn_w_up"], "wd": inputs["L0_ffn_w_down"],
    }
    base = {k: np.ascontiguousarray(v, dtype=np.float32) for k, v in base.items()}
    in_maps = [dict(base, x=np.ascontiguousarray(x[i])) for i in range(NCORES)]
    res = run_bass_kernel_spmd(nc, in_maps, core_ids=list(range(NCORES)))
    return np.stack([res.results[i]["out"] for i in range(NCORES)], axis=0)
